# version 06
# baseline (speedup 1.0000x reference)
import math
import numpy as np
import concourse.bass as bass
import concourse.mybir as mybir
from concourse.bass_utils import run_bass_kernel_spmd

F32 = mybir.dt.float32
BF16 = mybir.dt.bfloat16
ALU = mybir.AluOpType
ACT = mybir.ActivationFunctionType
AXX = mybir.AxisListType.X
MAGIC = 12582912.0
QMAX = 32767.0
EPS = 1e-5


class Cfg:
    def __init__(self, nseq=4, L=2048, tileA=256, tileB=512, ncore=8):
        self.nseq, self.L, self.ncore = nseq, L, ncore
        self.tileA, self.tileB = tileA, tileB
        self.T = nseq * L
        assert L % tileA == 0 and L % tileB == 0
        assert tileA % 128 == 0 and tileB % 128 == 0
        self.D, self.DI, self.DS, self.NH, self.HD = 1024, 2048, 64, 32, 64
        self.CONV, self.DIP, self.I = 2176, 4256, 3072


CFG = Cfg()
DEBUG_SPILL = False
STOP_PHASE = 99
STOP_A1 = 99


class Sem:
    __slots__ = ("name", "n", "h")

    def __init__(self, name):
        self.name, self.n, self.h = name, 0, None


class Sched:
    ENGINES = ("sync", "gpsimd", "tensor", "vector", "scalar")

    def __init__(self):
        self.ops = {e: [] for e in self.ENGINES}
        self.sems = []
        self.pending = {e: [] for e in self.ENGINES}

    def sem(self, name):
        s = Sem(name)
        self.sems.append(s)
        return s

    def barrier(self):
        snap = [(s, s.n) for s in self.sems if s.n > 0]
        for e in self.ENGINES:
            self.pending[e] = list(snap)

    def emit(self, eng, fn, waits=(), inc=None, amt=1):
        waits = list(self.pending[eng]) + [(s, v) for (s, v) in waits
                                           if s is not None and v > 0]
        self.pending[eng] = []
        tok = None
        if inc is not None:
            inc.n += amt
            tok = (inc, inc.n)

        def g(e, waits=tuple(waits), fn=fn, inc=inc, amt=amt):
            for (s, v) in waits:
                e.wait_ge(s.h, v)
            r = fn(e)
            if inc is not None:
                r.then_inc(inc.h, amt)
        self.ops[eng].append(g)
        return tok


class Arena:
    def __init__(self, nc, base, size):
        self.nc, self.base, self.size = nc, base, size
        self.off = 0
        self.uid = 0

    def reset(self):
        self.off = 0

    def alloc(self, name, shape, dt):
        nb = int(np.prod(shape[1:])) * mybir.dt.size(dt)
        nb = (nb + 31) // 32 * 32
        assert self.off + nb <= self.size, \
            f"arena overflow: {name} needs {nb} at off {self.off} / {self.size}"
        self.uid += 1
        t = self.nc.alloc_sbuf_tensor_at(
            f"ar{self.uid}_{name}", shape, dt, offset=self.base + self.off)
        self.off += nb
        return t


def build_program(cfg):
    nc = bass.Bass("TRN2", target_bir_lowering=False, detect_race_conditions=False)
    sc = Sched()
    T, L, nseq, TA, TB = cfg.T, cfg.L, cfg.nseq, cfg.tileA, cfg.tileB
    D, DI, NH, CONV, DIP, I = cfg.D, cfg.DI, cfg.NH, cfg.CONV, cfg.DIP, cfg.I
    NKD, NKI, NKC, NKF = D // 128, DI // 128, CONV // 128, I // 128
    NOC = (DIP + 127) // 128
    NCH = TA // 128
    ntA, ntB = T // TA, T // TB
    tps = L // TA

    def din(name, shape, dt=F32):
        return nc.dram_tensor(name, shape, dt, kind="ExternalInput").ap()

    x_hbm = din("x_fm", [128, NKD * T])
    win = din("win", [128, NKD * DIP])
    wup = din("wup", [128, NKF * D])
    wgate = din("wgate", [128, NKF * D])
    wdown = din("wdown", [128, NKD * I])
    woutp = din("woutp", [128, NKI * D])
    wtoh = din("wtoh", [128, NKD * D])
    convw_h = din("convw", [128, NKC * 4])
    convb_h = din("convb", [128, NKC])
    alog_h = din("alog", [128, 1])
    dvec_h = din("dvec", [128, NKI])
    dtb_h = din("dtb", [128, 1])
    normw_h = din("normw", [128, NKI])

    out_hbm = nc.dram_tensor("out_fm", [128, NKD * T], F32,
                             kind="ExternalOutput").ap()

    spk = dict(kind="ExternalOutput") if DEBUG_SPILL else {}
    xn_sp = nc.dram_tensor("xn_sp", [128, NKD * T], BF16, **spk).ap()
    xq_sp = nc.dram_tensor("xq_sp", [128, NKD * T], BF16, **spk).ap()
    yn_sp = nc.dram_tensor("yn_sp", [128, NKI * T], BF16, **spk).ap()
    s_sp = nc.dram_tensor("s_sp", [128, NKD * T], F32, **spk).ap()
    up_sp = nc.dram_tensor("up_sp", [128, NKF * T], F32, **spk).ap()
    gate_sp = nc.dram_tensor("gate_sp", [128, NKF * T], F32, **spk).ap()
    g2_sp = nc.dram_tensor("g2_sp", [128, NKF * T], F32, **spk).ap()
    o_sp = nc.dram_tensor("o_sp", [128, NKF * T], F32, **spk).ap()
    g_dbg = nc.dram_tensor("g_dbg", [128, NKD * T], F32, **spk).ap()
    oq_dbg = nc.dram_tensor("oq_dbg", [128, NKF * T], BF16, **spk).ap()
    wd_dbg = nc.dram_tensor("wd_dbg", [128, NKF * D], BF16, **spk).ap()
    sg_dbg = nc.dram_tensor("sg_dbg", [128, NKD * T], F32, **spk).ap()
    cc_in = [nc.dram_tensor(f"cc_in{k}", [1, 2], F32).ap() for k in range(4)]
    cc_out = [nc.dram_tensor(f"cc_out{k}", [1, 2], F32).ap() for k in range(4)]

    import contextlib
    ctx = contextlib.ExitStack()

    def sbt(name, shape, dt=F32):
        return nc.alloc_sbuf_tensor(name, shape, dt)

    ones1 = sbt("ones1", [1, 128])
    ones128 = sbt("ones128", [128, 1])
    id128 = sbt("id128", [128, 128])
    rowsel = sbt("rowsel", [32, NH * 128])
    pairsel = sbt("pairsel", [32, NKI * 128])
    csel = sbt("csel", [128, 64], BF16)
    bsel = sbt("bsel", [128, 64], BF16)
    ma = sbt("ma", [128, 128])
    zeros32 = sbt("zeros32", [32, 128])
    avec = sbt("avec", [128, 1])
    dvec = sbt("dvec_s", [128, NKI])
    dtbv = sbt("dtbv", [128, 1])
    normw = sbt("normw_s", [128, NKI])
    convw = sbt("convw_s", [128, NKC * 4])
    convb = sbt("convb_s", [128, NKC])
    c_one = sbt("c_one", [128, 1])
    c_eps = sbt("c_eps", [128, 1])
    scal = sbt("scal", [128, 12])
    runmax = sbt("runmax", [128, 8])
    rowA = sbt("rowA", [1, 256])
    rowB = sbt("rowB", [1, 256])
    rowC = sbt("rowC", [1, 256])
    ccrow = sbt("ccrow", [1, 2])

    WPOOL_COLS = NKD * DIP
    wpool = sbt("wpool", [128, WPOOL_COLS], BF16)

    def wsl(c0, n):
        assert c0 + n <= WPOOL_COLS
        return wpool[:, c0:c0 + n]

    ARENA = 108 * 1024
    rr = nc.bump_sbuf(ARENA)
    assert rr is not None, "no sbuf space for arena"
    ar = Arena(nc, rr[0], ARENA)

    pA = nc.alloc_psum_tensor("pA", [128, 1024], F32)
    pB = nc.alloc_psum_tensor("pB", [128, 1024], F32)
    pC = nc.alloc_psum_tensor("pC", [128, 1024], F32)
    g1 = nc.alloc_psum_tensor("g1", [128, 512], F32)
    g2p = nc.alloc_psum_tensor("g2p", [128, 512], F32)

    LD = sc.sem("LD")
    LDA = sc.sem("LDA")
    LDB = sc.sem("LDB")
    LDC = sc.sem("LDC")
    SPL = sc.sem("SPL")
    STG = sc.sem("STG")
    GD = sc.sem("GD")
    SD = sc.sem("SD")
    TN = sc.sem("TN")
    VC = sc.sem("VC")
    SCA = sc.sem("SCA")
    GP = sc.sem("GP")

    (SCL_IS0, SCL_S0, SCL_ISG, SCL_SG, SCL_ISUP, SCL_SUP,
     SCL_ISG2, SCL_SG2, SCL_ISO, SCL_SO) = range(10)
    RM_X, RM_UP, RM_G, RM_G2, RM_O, RM_T = range(6)

    def mm(out_ap, stat_ap, mov_ap, start, stop, waits=(), inc=None):
        return sc.emit(
            "tensor",
            lambda t, o=out_ap, s=stat_ap, m=mov_ap, a=start, b=stop:
                t.matmul(o, s, m, start=a, stop=b),
            waits=waits, inc=inc)

    def tr(out_ap, in_ap, idn, waits=(), inc=None):
        return sc.emit(
            "tensor",
            lambda t, o=out_ap, i=in_ap, d=idn: t.transpose(o, i, d),
            waits=waits, inc=inc)

    def vop(fn, waits=(), inc=None):
        return sc.emit("vector", fn, waits=waits, inc=inc or VC)

    def sop(fn, waits=(), inc=None):
        return sc.emit("scalar", fn, waits=waits, inc=inc or SCA)

    def gop(fn, waits=(), inc=None):
        return sc.emit("gpsimd", fn, waits=waits, inc=inc)

    def dma(eng, out_ap, in_ap, waits=(), sem=None):
        if sem is None:
            sem = {"sync": LD, "gpsimd": GD, "scalar": SD}[eng]
        return sc.emit(
            eng, lambda e, o=out_ap, i=in_ap: e.dma_start(out=o, in_=i),
            waits=waits, inc=sem, amt=16)

    def dmat(out_ap, in_ap, waits=()):
        return sc.emit(
            "scalar",
            lambda e, o=out_ap, i=in_ap: e.dma_start(out=o, in_=i, transpose=True),
            waits=waits, inc=SD, amt=16)

    def k3(hbm_ap, nk, t0, n):
        return hbm_ap.rearrange("p (k t) -> p k t", k=nk)[:, :, t0:t0 + n]

    def sb3(t, nk):
        return t[:, :].rearrange("p (k t) -> p k t", k=nk)

    gop(lambda g: g.memset(ones1[:, :], 1.0))
    gop(lambda g: g.memset(ones128[:, :], 1.0))
    gop(lambda g: g.memset(zeros32[:, :], 0.0))
    gop(lambda g: g.memset(c_one[:, :], 1.0))
    gop(lambda g: g.memset(c_eps[:, :], EPS))
    gop(lambda g: g.memset(ccrow[:, :], 0.0))
    gop(lambda g: g.memset(runmax[:, :], 0.0))
    gop(lambda g: g.memset(scal[:, :], 1.0))
    gop(lambda g: g.memset(id128[:, :], 1.0))
    gop(lambda g: g.affine_select(
        out=id128[:, :], in_=id128[:, :], compare_op=ALU.is_equal,
        fill=0.0, base=0, pattern=[[-1, 128]], channel_multiplier=1))
    gop(lambda g: g.memset(ma[:, :], 0.0))
    gop(lambda g: g.affine_select(
        out=ma[:, :], in_=ma[:, :], compare_op=ALU.is_ge,
        fill=-1e30, base=0, pattern=[[1, 128]], channel_multiplier=-1))
    gop(lambda g: g.memset(rowsel[:, :], 1.0))
    gop(lambda g: g.affine_select(
        out=rowsel[:, :].rearrange("p (h t) -> p h t", h=NH),
        in_=rowsel[:, :].rearrange("p (h t) -> p h t", h=NH),
        compare_op=ALU.is_equal, fill=0.0, base=0,
        pattern=[[-1, NH], [0, 128]], channel_multiplier=1))
    gop(lambda g: g.memset(pairsel[:, :], 1.0))
    gop(lambda g: g.affine_select(
        out=pairsel[:, :].rearrange("p (c u q) -> p c u q", c=NKI, u=2),
        in_=pairsel[:, :].rearrange("p (c u q) -> p c u q", c=NKI, u=2),
        compare_op=ALU.is_equal, fill=0.0, base=0,
        pattern=[[-2, NKI], [-1, 2], [0, 64]], channel_multiplier=1))
    gop(lambda g: g.memset(csel[:, :], 0.0))
    gop(lambda g: g.memset(csel[64:128, :], 1.0))
    gop(lambda g: g.affine_select(
        out=csel[64:128, :], in_=csel[64:128, :], compare_op=ALU.is_equal,
        fill=0.0, base=0, pattern=[[-1, 64]], channel_multiplier=1))
    gop(lambda g: g.memset(bsel[:, :], 0.0))
    gop(lambda g: g.memset(bsel[0:64, :], 1.0))
    gop(lambda g: g.affine_select(
        out=bsel[0:64, :], in_=bsel[0:64, :], compare_op=ALU.is_equal,
        fill=0.0, base=0, pattern=[[-1, 64]], channel_multiplier=1), inc=GP)

    for (dst, src) in ((avec, alog_h), (dvec, dvec_h), (dtbv, dtb_h),
                       (normw, normw_h), (convw, convw_h), (convb, convb_h)):
        dma("sync", dst[:, :], src)
    t_win = dma("gpsimd", wsl(0, NKD * DIP), win)
    sc.barrier()
    t_av = sop(lambda s: s.activation(avec[:, :], avec[:, :], ACT.Exp))
    vop(lambda v: v.tensor_scalar(avec[:, :], avec[:, :], -1.0, None, ALU.mult),
        waits=[t_av])
    sc.barrier()

    ar.reset()
    xt = ar.alloc("xt", [128, NKD * TA], F32)
    xnb = ar.alloc("xnb", [128, NKD * TA], BF16)
    tmpa = ar.alloc("tmpa", [128, 2 * TA], F32)
    zsil = ar.alloc("zsil", [128, NKI * TA], BF16)
    xbcr = ar.alloc("xbcr", [128, NKC * (TA + 4)], F32)
    xbcp = ar.alloc("xbcp", [128, NKC * TA], BF16)
    xtok = ar.alloc("xtok", [128, NKI * 128], BF16)
    mtmp = ar.alloc("mtmp", [128, 16 * 128], F32)
    msb = ar.alloc("msb", [128, 16 * 128], BF16)
    csb = ar.alloc("csb", [64, TA], F32)
    bsb = ar.alloc("bsb", [128, TA], F32)
    cbsb = ar.alloc("cbsb", [128, 128], F32)
    btok = ar.alloc("btok", [128, 64], BF16)
    dtsb = ar.alloc("dtsb", [128, TA], F32)
    ssb = ar.alloc("ssb", [128, TA], F32)
    usb = ar.alloc("usb", [128, TA], F32)
    esb = ar.alloc("esb", [128, 128], F32)
    lam = ar.alloc("lam", [128, 128], F32)
    ut_t = ar.alloc("ut", [128, 32], F32)
    lt_t = ar.alloc("lt", [128, 32], F32)
    da_row = ar.alloc("da", [1, 32], F32)
    lamx = ar.alloc("lamx", [128, NKI * 128], BF16)
    hsb = ar.alloc("hsb", [64, DI], F32)
    ysb = ar.alloc("ysb", [128, NKI * 128], F32)
    ytmp = ar.alloc("ytmp", [128, 8 * 128], F32)
    ynb = ar.alloc("ynb", [128, NKI * TA], BF16)

    for it in range(ntA):
        t0 = it * TA
        seq_start = (it % tps == 0)

        t_x = dma("sync", sb3(xt, NKD), k3(x_hbm, NKD, t0, TA), sem=LDA)
        if it == 0:
            for buf in (dtsb, ssb, usb, bsb):
                vop(lambda v, b=buf: v.memset(b[:, :], 0.0))
            vop(lambda v: v.memset(esb[:, :], 0.0))
            vop(lambda v: v.memset(lam[:, :], 0.0))

        t_s1 = t_s2 = None
        sqmm = {}
        for k in range(NKD):
            slot = k % 2
            wsq = [t_x]
            if k >= 2:
                wsq.append(sqmm[k - 2])
            t_q = sop(lambda s, k=k, slot=slot: s.activation(
                tmpa[:, slot * TA:(slot + 1) * TA],
                xt[:, k * TA:(k + 1) * TA], ACT.Square), waits=wsq)
            t_s2 = mm(pB[0:1, 0:TA], ones128[:, :], tmpa[:, slot * TA:(slot + 1) * TA],
                      start=(k == 0), stop=(k == NKD - 1), waits=[t_q],
                      inc=TN)
            sqmm[k] = t_s2
            t_s1 = mm(pA[0:1, 0:TA], ones128[:, :], xt[:, k * TA:(k + 1) * TA],
                      start=(k == 0), stop=(k == NKD - 1),
                      waits=[t_x] if k == 0 else (),
                      inc=TN if k == NKD - 1 else None)
        t_m = vop(lambda v: v.tensor_scalar(rowA[0:1, 0:TA], pA[0:1, 0:TA],
                                            1.0 / D, None, ALU.mult), waits=[t_s1])
        t_m2 = vop(lambda v: v.tensor_mul(rowB[0:1, 0:TA], rowA[0:1, 0:TA],
                                          rowA[0:1, 0:TA]), waits=[t_m])
        t_var = vop(lambda v: v.scalar_tensor_tensor(
            rowC[0:1, 0:TA], pB[0:1, 0:TA], 1.0 / D, rowB[0:1, 0:TA],
            ALU.mult, ALU.subtract), waits=[t_s2, t_m2])
        t_sd = sop(lambda s: s.activation(rowC[0:1, 0:TA], rowC[0:1, 0:TA],
                                          ACT.Sqrt, bias=c_eps[0:1, :]),
                   waits=[t_var])
        t_rstd = vop(lambda v: v.reciprocal(rowC[0:1, 0:TA], rowC[0:1, 0:TA]),
                     waits=[t_sd])
        t_mb = vop(lambda v: v.scalar_tensor_tensor(
            rowB[0:1, 0:TA], rowA[0:1, 0:TA], -1.0, rowC[0:1, 0:TA],
            ALU.mult, ALU.mult), waits=[t_rstd])
        t_bs = mm(g1[:, 0:TA], ones1[:, :], rowC[0:1, 0:TA], True, True,
                  waits=[t_rstd], inc=TN)
        t_bb = mm(g2p[:, 0:TA], ones1[:, :], rowB[0:1, 0:TA], True, True,
                  waits=[t_mb], inc=TN)
        t_xn = None
        for k in range(NKD):
            vop(lambda v, k=k: v.tensor_mul(
                xt[:, k * TA:(k + 1) * TA], xt[:, k * TA:(k + 1) * TA],
                g1[:, 0:TA]), waits=[t_bs, t_bb] if k == 0 else ())
            t_xn = vop(lambda v, k=k: v.tensor_add(
                xnb[:, k * TA:(k + 1) * TA], xt[:, k * TA:(k + 1) * TA],
                g2p[:, 0:TA]))
        t_r = vop(lambda v: v.reduce_max(runmax[:, RM_T:RM_T + 1], xnb[:, :],
                                         axis=AXX, apply_absolute_value=True),
                  waits=[t_xn])
        if it == 0:
            vop(lambda v: v.tensor_copy(runmax[:, RM_X:RM_X + 1],
                                        runmax[:, RM_T:RM_T + 1]), waits=[t_r])
        else:
            vop(lambda v: v.tensor_max(runmax[:, RM_X:RM_X + 1],
                                       runmax[:, RM_X:RM_X + 1],
                                       runmax[:, RM_T:RM_T + 1]), waits=[t_r])
        dma("sync", k3(xn_sp, NKD, t0, TA), sb3(xnb, NKD), waits=[t_xn], sem=SPL)

        if STOP_A1 < 2:
            continue
        ipev = {}
        t_dt = None
        for oc in range(NOC):
            ow = min(128, DIP - oc * 128)
            pb = g1 if oc % 2 == 0 else g2p
            last = None
            for k in range(NKD):
                wts = []
                if k == 0:
                    wts = [t_xn]
                    if oc >= 2:
                        wts.append(ipev[oc - 2])
                last = mm(pb[0:ow, 0:TA], wsl(k * DIP + oc * 128, ow),
                          xnb[:, k * TA:(k + 1) * TA],
                          start=(k == 0), stop=(k == NKD - 1),
                          waits=wts, inc=TN if k == NKD - 1 else None)
            if oc < NKI:
                ev = sop(lambda s, oc=oc, pb=pb: s.activation(
                    zsil[:, oc * TA:(oc + 1) * TA], pb[:, 0:TA], ACT.Silu),
                    waits=[last])
            elif oc < NKI + NKC:
                c = oc - NKI
                ev = vop(lambda v, c=c, pb=pb: v.tensor_copy(
                    xbcr[:, c * (TA + 4) + 4: c * (TA + 4) + 4 + TA],
                    pb[:, 0:TA]), waits=[last])
            else:
                ev = sop(lambda s, pb=pb: s.activation(
                    dtsb[0:32, 0:TA], pb[0:32, 0:TA], ACT.Exp,
                    bias=dtbv[0:32, :]), waits=[last])
                t_dt = sop(lambda s: s.activation(
                    dtsb[0:32, 0:TA], dtsb[0:32, 0:TA], ACT.Ln,
                    bias=c_one[0:32, :]), waits=[ev])
                ev = t_dt
            ipev[oc] = ev

        if STOP_A1 < 3:
            continue
        if seq_start:
            t_halo = vop(lambda v: v.memset(
                xbcr[:, :].rearrange("p (c t) -> p c t", c=NKC)[:, :, 0:4], 0.0))
        else:
            t_halo = None
        convk = {}
        for c in range(NKC):
            base = c * (TA + 4)
            slot = c % 2
            wcv = [ipev[NKI + c]]
            if t_halo is not None:
                wcv.append(t_halo)
            if c >= 2:
                wcv.append(convk[c - 2])
            tt = vop(lambda v, base=base, c=c, slot=slot: v.tensor_scalar(
                xt[:, slot * TA:(slot + 1) * TA],
                xbcr[:, base + 1: base + 1 + TA],
                convw[:, 4 * c:4 * c + 1], None, ALU.mult), waits=wcv)
            for tap in (1, 2, 3):
                tt = vop(lambda v, base=base, c=c, tap=tap, slot=slot:
                         v.scalar_tensor_tensor(
                             xt[:, slot * TA:(slot + 1) * TA],
                             xbcr[:, base + 1 + tap: base + 1 + tap + TA],
                             convw[:, 4 * c + tap:4 * c + tap + 1],
                             xt[:, slot * TA:(slot + 1) * TA],
                             ALU.mult, ALU.add))
            convk[c] = sop(lambda s, c=c, slot=slot: s.activation(
                xbcp[:, c * TA:(c + 1) * TA],
                xt[:, slot * TA:(slot + 1) * TA], ACT.Silu,
                bias=convb[:, c:c + 1]), waits=[tt])
        t_halo_n = vop(lambda v: v.tensor_copy(
            xbcr[:, :].rearrange("p (c t) -> p c t", c=NKC)[:, :, 0:4],
            xbcr[:, :].rearrange("p (c t) -> p c t", c=NKC)[:, :, TA:TA + 4]),
            waits=[convk[NKC - 1]])

        if STOP_A1 < 4:
            continue
        t_a = vop(lambda v: v.tensor_scalar(
            ssb[0:32, 0:TA], dtsb[0:32, 0:TA], avec[0:32, 0:1], None, ALU.mult),
            waits=[t_dt])
        t_s = None
        for c in range(NCH):
            t_s = vop(lambda v, c=c: v.tensor_tensor_scan(
                ssb[0:32, c * 128:(c + 1) * 128],
                ssb[0:32, c * 128:(c + 1) * 128],
                zeros32[:, :], 0.0, ALU.add, ALU.add),
                waits=[t_a] if c == 0 else ())
        t_lndt = sop(lambda s: s.activation(usb[0:32, 0:TA], dtsb[0:32, 0:TA],
                                            ACT.Ln), waits=[t_dt])
        t_u = vop(lambda v: v.tensor_sub(usb[0:32, 0:TA], ssb[0:32, 0:TA],
                                         usb[0:32, 0:TA]), waits=[t_lndt, t_s])

        if STOP_A1 < 5:
            continue
        t_ce = mm(pC[0:64, 0:TA], csel[:, :], xbcp[:, 16 * TA:17 * TA],
                  True, True, waits=[convk[16]], inc=TN)
        t_cs = vop(lambda v: v.tensor_copy(csb[:, 0:TA], pC[0:64, 0:TA]),
                   waits=[t_ce])
        t_be = mm(pC[0:64, 0:TA], bsel[:, :], xbcp[:, 16 * TA:17 * TA],
                  True, True, waits=[t_cs], inc=TN)
        t_bs2 = vop(lambda v: v.tensor_copy(bsb[0:64, 0:TA], pC[0:64, 0:TA]),
                    waits=[t_be])

        if seq_start:
            t_h0 = vop(lambda v: v.memset(hsb[:, :], 0.0))
        else:
            t_h0 = None

        for cch in range(NCH):
            tc0 = cch * 128
            sc.barrier()
            if STOP_A1 < 6:
                continue
            t_xtk = None
            for fc in range(NKI):
                t_xtk = dmat(xtok[:, fc * 128:(fc + 1) * 128],
                             xbcp[:, fc * TA + tc0: fc * TA + tc0 + 128],
                             waits=[convk[fc]] if fc == 0 else ())
            if STOP_A1 < 7:
                continue
            t_ut = tr(g1[0:128, 0:128], usb[:, tc0:tc0 + 128],
                      id128[:, :], waits=[t_u], inc=TN)
            t_le = vop(lambda v, tc0=tc0: v.tensor_scalar(
                lam[0:32, :], ssb[0:32, tc0:tc0 + 128],
                ssb[0:32, tc0 + 127:tc0 + 128], None, ALU.subtract), waits=[t_s])
            t_le = sop(lambda s: s.activation(lam[0:32, :], lam[0:32, :], ACT.Exp,
                                              scale=-1.0), waits=[t_le])
            t_le = vop(lambda v, tc0=tc0: v.tensor_mul(
                lam[0:32, :], lam[0:32, :], dtsb[0:32, tc0:tc0 + 128]),
                waits=[t_le])
            t_ee = sop(lambda s, tc0=tc0: s.activation(
                esb[0:32, :], ssb[0:32, tc0:tc0 + 128], ACT.Exp), waits=[t_s])
            t_lt = tr(g1[0:128, 128:256], lam[:, :], id128[:, :],
                      waits=[t_le], inc=TN)
            t_cb = mm(g1[0:128, 256:384], bsb[0:64, tc0:tc0 + 128],
                      csb[:, tc0:tc0 + 128], True, True, waits=[t_bs2], inc=TN)
            t_da = tr(g2p[0:1, 256:384], esb[:, 127:128], id128[:, :],
                      waits=[t_ee], inc=TN)
            t_bt = tr(g2p[0:128, 128:256], bsb[:, tc0:tc0 + 128],
                      id128[:, :], waits=[t_bs2], inc=TN)
            t_ute = vop(lambda v: v.tensor_copy(ut_t[:, :], g1[0:128, 0:32]),
                        waits=[t_cb])
            t_lte = vop(lambda v: v.tensor_copy(lt_t[:, :], g1[0:128, 128:160]))
            t_cbe = vop(lambda v: v.tensor_copy(cbsb[:, :], g1[0:128, 256:384]))
            t_dae = vop(lambda v: v.tensor_copy(da_row[:, :], g2p[0:1, 256:288]),
                        waits=[t_bt])
            t_bte = vop(lambda v: v.tensor_copy(btok[:, :], g2p[0:128, 128:192]))
            t_lx = vop(lambda v: v.tensor_tensor(
                lamx[:, :], xtok[:, :],
                lt_t[:, :].unsqueeze(2).broadcast_to([128, 32, 64]), ALU.mult),
                waits=[t_lte, t_xtk])

            if STOP_A1 < 8:
                continue
            t_yb_all = []
            t_half = None
            stp_prev = {}
            m_prev = None
            ya_prev = None
            for half in range(2):
                h0, f0 = half * 16, half * 8
                t_st1 = None
                for hh in range(16):
                    h = h0 + hh
                    slot = hh % 2
                    sl_ap = (g1[0:128, 384:512] if slot == 0 else g2p[0:128, 0:128])
                    wst = [t_s]
                    if hh <= 1 and half == 0:
                        wst += [t_ute, t_lte, t_cbe, t_dae, t_bte]
                    if slot in stp_prev:
                        wst.append(stp_prev[slot])
                    t_sp = mm(sl_ap, rowsel[:, h * 128:(h + 1) * 128],
                              ssb[0:32, tc0:tc0 + 128], True, True,
                              waits=wst, inc=TN)
                    wst1 = [t_sp, t_ute]
                    if hh == 0 and m_prev is not None:
                        wst1.append(m_prev)
                    t_st1 = vop(lambda v, hh=hh, sl_ap=sl_ap, h=h:
                                v.scalar_tensor_tensor(
                                    mtmp[:, hh * 128:(hh + 1) * 128], sl_ap,
                                    ut_t[:, h:h + 1], ma[:, :],
                                    ALU.subtract, ALU.add), waits=wst1)
                    stp_prev[slot] = t_st1
                t_exp = sop(lambda s: s.activation(mtmp[:, :], mtmp[:, :],
                                                   ACT.Exp), waits=[t_st1])
                wm = [t_exp, t_cbe]
                if ya_prev is not None:
                    wm.append(ya_prev)
                t_m = vop(lambda v: v.tensor_tensor(
                    msb[:, :], mtmp[:, :],
                    cbsb[:, :].unsqueeze(1).broadcast_to([128, 16, 128]),
                    ALU.mult), waits=wm)
                m_prev = t_m

                if STOP_A1 < 9:
                    continue
                wtrio = [t_half] if t_half is not None else []
                t_ya = None
                for hh in range(16):
                    h = h0 + hh
                    fm, bp = hh // 2, 64 * (hh % 2)
                    t_ya = mm(pA[bp:bp + 64, fm * 128:(fm + 1) * 128],
                              xtok[:, h * 64:(h + 1) * 64],
                              msb[:, hh * 128:(hh + 1) * 128], True, True,
                              waits=[t_m] + ([t_xtk] + wtrio if hh == 0 else []),
                              inc=TN)
                ya_prev = t_ya
                t_yb = None
                for fm in range(8):
                    fc = f0 + fm
                    wyb = []
                    if fm == 0:
                        wyb = [t_cs] + wtrio
                        if t_h0 is not None and cch == 0:
                            wyb.append(t_h0)
                    t_yb = mm(pB[:, fm * 128:(fm + 1) * 128],
                              hsb[:, fc * 128:(fc + 1) * 128],
                              csb[:, tc0:tc0 + 128], True, True,
                              waits=wyb, inc=TN)
                t_yb_all.append(t_yb)
                t_yc = None
                for fm in range(8):
                    fc = f0 + fm
                    t_yc = mm(pC[:, fm * 128:(fm + 1) * 128],
                              pairsel[:, fc * 128:(fc + 1) * 128],
                              esb[0:32, :], True, True,
                              waits=([t_ee] + wtrio) if fm == 0 else (), inc=TN)

                if STOP_A1 < 10:
                    continue
                t_c0 = vop(lambda v, f0=f0: v.tensor_copy(
                    ysb[:, f0 * 128:(f0 + 8) * 128], pC[:, 0:1024]),
                    waits=[t_yc])
                t_c1 = vop(lambda v, f0=f0: v.tensor_mul(
                    ytmp[:, :], pB[:, 0:1024],
                    ysb[:, f0 * 128:(f0 + 8) * 128]), waits=[t_yb, t_c0])
                t_c2 = vop(lambda v, f0=f0: v.tensor_add(
                    ysb[:, f0 * 128:(f0 + 8) * 128], ytmp[:, :], pA[:, 0:1024]),
                    waits=[t_ya, t_c1])
                t_c3 = vop(lambda v, f0=f0, tc0=tc0: v.tensor_tensor(
                    ytmp[:, :],
                    xbcp[:, :].rearrange("p (c t) -> p c t", c=NKC)[:, f0:f0 + 8, tc0:tc0 + 128],
                    dvec[:, f0:f0 + 8].unsqueeze(2).broadcast_to([128, 8, 128]),
                    ALU.mult), waits=[t_c2])
                t_c4 = vop(lambda v, f0=f0: v.tensor_add(
                    ysb[:, f0 * 128:(f0 + 8) * 128],
                    ysb[:, f0 * 128:(f0 + 8) * 128], ytmp[:, :]))
                t_half = vop(lambda v, f0=f0, tc0=tc0: v.tensor_mul(
                    ysb[:, f0 * 128:(f0 + 8) * 128],
                    ysb[:, f0 * 128:(f0 + 8) * 128],
                    zsil[:, :].rearrange("p (c t) -> p c t", c=NKI)[:, f0:f0 + 8, tc0:tc0 + 128]))

            if STOP_A1 < 11:
                continue
            t_hu = []
            for hf in range(2):
                pd = pA if hf == 0 else pB
                t_um = None
                for q in range(2):
                    t_um = mm(pd[0:64, q * 512:(q + 1) * 512], btok[:, :],
                              lamx[:, (hf * 2 + q) * 512:(hf * 2 + q + 1) * 512],
                              True, True,
                              waits=[t_bte, t_lx, t_half] if q == 0 else (),
                              inc=TN)
                t_hu.append(t_um)
            t_hs1 = None
            for hf in range(2):
                pd = pA if hf == 0 else pB
                for qq in range(2):
                    t_db = mm(pC[0:64, qq * 512:(qq + 1) * 512],
                              ones1[0:1, 0:64],
                              da_row[0:1, hf * 16 + qq * 8: hf * 16 + (qq + 1) * 8]
                              .unsqueeze(2).broadcast_to([1, 8, 64]),
                              True, True,
                              waits=([t_dae, t_half] if hf == 0 else [t_hs0])
                              if qq == 0 else (), inc=TN)
                t_hs0 = vop(lambda v, hf=hf: v.tensor_mul(
                    hsb[:, hf * 1024:(hf + 1) * 1024],
                    hsb[:, hf * 1024:(hf + 1) * 1024], pC[0:64, 0:1024]),
                    waits=[t_db, t_yb_all[0], t_yb_all[1]])
                t_hs1 = vop(lambda v, hf=hf: v.tensor_add(
                    hsb[:, hf * 1024:(hf + 1) * 1024],
                    hsb[:, hf * 1024:(hf + 1) * 1024], pd[0:64, 0:1024]),
                    waits=[t_hu[hf]])

            if STOP_A1 < 12:
                continue
            t_q2 = sop(lambda s: s.activation(mtmp[:, :], ysb[:, 0:2048], ACT.Square),
                       waits=[t_half, t_hs1])
            t_r2 = None
            for k2 in range(NKI):
                t_r2 = mm(g2p[0:1, 128:128 + 128], ones128[:, :],
                          mtmp[:, k2 * 128:(k2 + 1) * 128],
                          start=(k2 == 0), stop=(k2 == NKI - 1),
                          waits=[t_q2] if k2 == 0 else (),
                          inc=TN if k2 == NKI - 1 else None)
            t_rs = vop(lambda v: v.tensor_scalar(
                rowA[0:1, 0:128], g2p[0:1, 128:256], 1.0 / DI, None, ALU.mult),
                waits=[t_r2])
            t_rs = sop(lambda s: s.activation(rowA[0:1, 0:128], rowA[0:1, 0:128],
                                              ACT.Sqrt, bias=c_eps[0:1, :]),
                       waits=[t_rs])
            t_rs = vop(lambda v: v.reciprocal(rowA[0:1, 0:128], rowA[0:1, 0:128]),
                       waits=[t_rs])
            t_rb = mm(g2p[:, 256:384], ones1[:, :], rowA[0:1, 0:128], True, True,
                      waits=[t_rs], inc=TN)
            t_yn = vop(lambda v, tc0=tc0: v.scalar_tensor_tensor(
                ysb[:, 0:2048], ysb[:, 0:2048], 1.0,
                g2p[:, 256:384].unsqueeze(1).broadcast_to([128, 16, 128]),
                ALU.mult, ALU.mult), waits=[t_rb])
            t_yn = vop(lambda v, tc0=tc0: v.tensor_tensor(
                ynb[:, :].rearrange("p (c t) -> p c t", c=NKI)[:, :, tc0:tc0 + 128],
                ysb[:, :].rearrange("p (c t) -> p c t", c=NKI)[:, :, 0:128],
                normw[:, :].unsqueeze(2).broadcast_to([128, NKI, 128]),
                ALU.mult), waits=[t_yn])
            t_yn_last = t_yn
        if STOP_A1 >= 12:
            dma("sync", k3(yn_sp, NKI, t0, TA), sb3(ynb, NKI),
                waits=[t_yn_last], sem=SPL)
        sc.barrier()

    def do_cc(k, pairs):
        sc.barrier()
        tt = None
        for i, (rm, _, _) in enumerate(pairs):
            tt = tr(g1[0:1, 0:128], runmax[:, rm:rm + 1], id128[:, :],
                    waits=[tt] if tt else (), inc=TN)
            tt = vop(lambda v, i=i: v.reduce_max(
                ccrow[0:1, i:i + 1], g1[0:1, 0:128], axis=AXX), waits=[tt])
        t_d = dma("gpsimd", cc_in[k], ccrow[0:1, 0:2], waits=[tt])
        t_c = gop(lambda g, k=k: g.collective_compute(
            "AllReduce", ALU.max, replica_groups=[list(range(cfg.ncore))],
            ins=[cc_in[k].opt()], outs=[cc_out[k].opt()]),
            waits=[t_d], inc=GP)
        t_b = dma("gpsimd", ccrow[0:1, 0:2], cc_out[k], waits=[t_c])
        tt = vop(lambda v: v.tensor_scalar(
            ccrow[0:1, 0:2], ccrow[0:1, 0:2], 1.0 / QMAX, 1e-12,
            ALU.mult, ALU.max), waits=[t_b])
        tt = mm(g1[:, 0:2], ones1[:, :], ccrow[0:1, 0:2], True, True,
                waits=[tt], inc=TN)
        toks = []
        for i, (_, s_col, i_col) in enumerate(pairs):
            t1 = vop(lambda v, i=i, s_col=s_col: v.tensor_copy(
                scal[:, s_col:s_col + 1], g1[:, i:i + 1]), waits=[tt])
            t2 = vop(lambda v, i=i, i_col=i_col: v.reciprocal(
                scal[:, i_col:i_col + 1], g1[:, i:i + 1]), waits=[tt])
            toks += [t1, t2]
        sc.barrier()
        return toks

    if STOP_PHASE >= 2:
        do_cc(0, [(RM_X, SCL_S0, SCL_IS0)])

    if STOP_PHASE >= 3:
        ar.reset()
        ynb2 = ar.alloc("ynb2", [128, NKI * TB], BF16)
        hbf = ar.alloc("hbf", [128, NKD * TB], BF16)
        sbf = ar.alloc("sbf", [128, NKD * TB], F32)
        t_w2 = dma("gpsimd", wsl(0, NKI * D), woutp)
        t_w3 = dma("gpsimd", wsl(NKI * D, NKD * D), wtoh)
        for it in range(ntB):
            t0 = it * TB
            t_y = dma("sync", sb3(ynb2, NKI), k3(yn_sp, NKI, t0, TB), sem=LDA)
            ipev = {}
            for oc in range(NKD):
                pb = g1 if oc % 2 == 0 else g2p
                last = None
                for k in range(NKI):
                    wts = []
                    if k == 0:
                        wts = [t_y, t_w3]
                        if oc >= 2:
                            wts.append(ipev[oc - 2])
                    last = mm(pb[:, 0:TB], wsl(k * D + oc * 128, 128),
                              ynb2[:, k * TB:(k + 1) * TB],
                              start=(k == 0), stop=(k == NKI - 1),
                              waits=wts, inc=TN if k == NKI - 1 else None)
                ipev[oc] = sop(lambda s, oc=oc, pb=pb: s.activation(
                    hbf[:, oc * TB:(oc + 1) * TB], pb[:, 0:TB], ACT.Identity),
                    waits=[last])
            t_hb = ipev[NKD - 1]
            spev = {}
            for oc in range(NKD):
                pb = g1 if oc % 2 == 0 else g2p
                last = None
                for k in range(NKD):
                    wts = []
                    if k == 0:
                        wts = [t_hb, t_w3, ipev[NKD - 2]]
                        if oc >= 2:
                            wts.append(spev[oc - 2])
                    last = mm(pb[:, 0:TB], wsl(NKI * D + k * D + oc * 128, 128),
                              hbf[:, k * TB:(k + 1) * TB],
                              start=(k == 0), stop=(k == NKD - 1),
                              waits=wts, inc=TN if k == NKD - 1 else None)
                spev[oc] = vop(lambda v, oc=oc, pb=pb: v.tensor_copy(
                    sbf[:, oc * TB:(oc + 1) * TB], pb[:, 0:TB]), waits=[last])
            dma("sync", k3(s_sp, NKD, t0, TB), sb3(sbf, NKD), waits=[spev[NKD - 1]], sem=SPL)
            sc.barrier()

    def lpbq_load(w_hbm, nO, nI, wrow, wsc, wqb):
        nblk = nI // 32
        O = nO * 128
        t_tr = None
        prev_tr = None
        for oc in range(nO):
            wl = [prev_tr] if prev_tr is not None else []
            t_l = dma("sync", wrow[:, :],
                      w_hbm.rearrange("p (c i) -> p c i", c=nO)[:, oc, :],
                      waits=wl, sem=LDB)
            t1 = vop(lambda v: v.reduce_max(
                wsc[:, 0:nblk],
                wrow[:, :].rearrange("p (b q) -> p b q", q=32),
                axis=AXX, apply_absolute_value=True), waits=[t_l])
            t1 = vop(lambda v: v.tensor_scalar(
                wsc[:, 0:nblk], wsc[:, 0:nblk], 1.0 / 7.0, 1e-12,
                ALU.mult, ALU.max), waits=[t1])
            t1 = vop(lambda v: v.reciprocal(wsc[:, nblk:2 * nblk], wsc[:, 0:nblk]),
                     waits=[t1])
            t2 = vop(lambda v: v.tensor_tensor(
                wrow[:, :], wrow[:, :],
                wsc[:, nblk:2 * nblk].unsqueeze(2).broadcast_to([128, nblk, 32]),
                ALU.mult), waits=[t1])
            t2 = vop(lambda v: v.tensor_scalar(
                wrow[:, :], wrow[:, :], MAGIC, MAGIC, ALU.add, ALU.subtract),
                waits=[t2])
            t2 = vop(lambda v: v.tensor_tensor(
                wqb[:, :], wrow[:, :],
                wsc[:, 0:nblk].unsqueeze(2).broadcast_to([128, nblk, 32]),
                ALU.mult), waits=[t2])
            for kc in range(nI // 128):
                t_tr = dmat(wsl(kc * O + oc * 128, 128),
                            wqb[:, kc * 128:(kc + 1) * 128],
                            waits=[t2] if kc == 0 else ())
            prev_tr = t_tr
        return t_tr

    def phase_b(which, w_hbm, spill, rm_col):
        sc.barrier()
        ar.reset()
        xnb2 = ar.alloc("xnb2", [128, NKD * TB], BF16)
        xqf = ar.alloc("xqf", [128, NKD * TB], F32)
        xqb = ar.alloc("xqb", [128, NKD * TB], BF16)
        stg = ar.alloc("stg", [128, 4 * TB], F32)
        wrow = ar.alloc("wrow", [128, D], F32)
        wsc = ar.alloc("wsc", [128, 2 * (D // 32)], F32)
        wqb = ar.alloc("wqb", [128, D], BF16)
        t_w = lpbq_load(w_hbm, NKF, D, wrow, wsc, wqb)
        first = True
        for it in range(ntB):
            t0 = it * TB
            if which == "up":
                t_in = dma("sync", sb3(xnb2, NKD), k3(xn_sp, NKD, t0, TB), sem=LDA)
                t1 = sop(lambda s: s.activation(
                    xqf[:, :], xnb2[:, :], ACT.Identity,
                    scale=scal[:, SCL_IS0:SCL_IS0 + 1]), waits=[t_in])
                t2 = vop(lambda v: v.tensor_scalar(
                    xqf[:, :], xqf[:, :], MAGIC, MAGIC, ALU.add, ALU.subtract),
                    waits=[t1])
                t_q = sop(lambda s: s.activation(
                    xqb[:, :], xqf[:, :], ACT.Identity,
                    scale=scal[:, SCL_S0:SCL_S0 + 1]), waits=[t2])
                dma("sync", k3(xq_sp, NKD, t0, TB), sb3(xqb, NKD), waits=[t_q], sem=SPL)
            else:
                t_q = dma("sync", sb3(xqb, NKD), k3(xq_sp, NKD, t0, TB), sem=LDA)
            ipev = {}
            stgt = {}
            for oc in range(NKF):
                pb = g1 if oc % 2 == 0 else g2p
                slot = oc % 4
                last = None
                for k in range(NKD):
                    wts = []
                    if k == 0:
                        wts = [t_q] + ([t_w] if first else [])
                        if oc >= 2:
                            wts.append(ipev[oc - 2])
                    last = mm(pb[:, 0:TB], wsl(k * I + oc * 128, 128),
                              xqb[:, k * TB:(k + 1) * TB],
                              start=(k == 0), stop=(k == NKD - 1),
                              waits=wts, inc=TN if k == NKD - 1 else None)
                first = False
                wev = [last]
                if ("stg", slot) in stgt:
                    wev.append(stgt[("stg", slot)])
                ev = vop(lambda v, pb=pb, slot=slot: v.tensor_copy(
                    stg[:, slot * TB:(slot + 1) * TB], pb[:, 0:TB]), waits=wev)
                ipev[oc] = ev
                t_mr = vop(lambda v, slot=slot: v.reduce_max(
                    runmax[:, RM_T:RM_T + 1], stg[:, slot * TB:(slot + 1) * TB],
                    axis=AXX, apply_absolute_value=True), waits=[ev])
                if it == 0 and oc == 0:
                    vop(lambda v: v.tensor_copy(
                        runmax[:, rm_col:rm_col + 1], runmax[:, RM_T:RM_T + 1]),
                        waits=[t_mr])
                else:
                    vop(lambda v: v.tensor_max(
                        runmax[:, rm_col:rm_col + 1],
                        runmax[:, rm_col:rm_col + 1],
                        runmax[:, RM_T:RM_T + 1]), waits=[t_mr])
                if slot == 3:
                    td = dma("sync",
                             spill.rearrange("p (c t) -> p c t", c=NKF)[:, oc - 3:oc + 1, t0:t0 + TB],
                             stg[:, :].rearrange("p (c t) -> p c t", c=4),
                             waits=[ipev[oc]], sem=STG)
                    for sl in range(4):
                        stgt[("stg", sl)] = td
            sc.barrier()
        sc.barrier()

    if STOP_PHASE >= 4:
        phase_b("up", wup, up_sp, RM_UP)
    if STOP_PHASE >= 5:
        phase_b("gate", wgate, gate_sp, RM_G)
    if STOP_PHASE >= 5:
        do_cc(1, [(RM_UP, SCL_SUP, SCL_ISUP), (RM_G, SCL_SG, SCL_ISG)])

    if STOP_PHASE >= 6:
        ar.reset()
        big1 = ar.alloc("big1", [128, NKF * TB], F32)
        big2 = ar.alloc("big2", [128, NKF * TB], F32)
        for it in range(ntB):
            t0 = it * TB
            t_in = dma("sync", sb3(big1, NKF), k3(gate_sp, NKF, t0, TB), sem=LDA)
            t1 = vop(lambda v: v.tensor_scalar(
                big2[:, :], big1[:, :], scal[:, SCL_ISG:SCL_ISG + 1], MAGIC,
                ALU.mult, ALU.add), waits=[t_in])
            t1 = vop(lambda v: v.tensor_scalar(
                big2[:, :], big2[:, :], MAGIC, scal[:, SCL_SG:SCL_SG + 1],
                ALU.subtract, ALU.mult), waits=[t1])
            t2 = sop(lambda s: s.activation(big1[:, :], big2[:, :], ACT.Sigmoid),
                     waits=[t1])
            t2 = vop(lambda v: v.tensor_scalar(
                big1[:, :], big1[:, :], 65536.0, MAGIC, ALU.mult, ALU.add),
                waits=[t2])
            t2 = vop(lambda v: v.tensor_scalar(
                big1[:, :], big1[:, :], MAGIC, 65535.0, ALU.subtract, ALU.min),
                waits=[t2])
            t3 = vop(lambda v: v.scalar_tensor_tensor(
                big2[:, :], big2[:, :], 1.0 / 65536.0, big1[:, :],
                ALU.mult, ALU.mult), waits=[t2])
            t_mr = vop(lambda v: v.reduce_max(runmax[:, RM_T:RM_T + 1], big2[:, :],
                                              axis=AXX, apply_absolute_value=True),
                       waits=[t3])
            if it == 0:
                vop(lambda v: v.tensor_copy(runmax[:, RM_G2:RM_G2 + 1],
                                            runmax[:, RM_T:RM_T + 1]), waits=[t_mr])
            else:
                vop(lambda v: v.tensor_max(runmax[:, RM_G2:RM_G2 + 1],
                                           runmax[:, RM_G2:RM_G2 + 1],
                                           runmax[:, RM_T:RM_T + 1]), waits=[t_mr])
            dma("sync", k3(g2_sp, NKF, t0, TB), sb3(big2, NKF), waits=[t3], sem=SPL)
            sc.barrier()
    if STOP_PHASE >= 6:
        do_cc(2, [(RM_G2, SCL_SG2, SCL_ISG2)])

    if STOP_PHASE >= 7:
        for it in range(ntB):
            t0 = it * TB
            t_a1 = dma("sync", sb3(big1, NKF), k3(g2_sp, NKF, t0, TB), sem=LDA)
            t_a2 = dma("sync", sb3(big2, NKF), k3(up_sp, NKF, t0, TB), sem=LDB)
            t1 = vop(lambda v: v.tensor_scalar(
                big1[:, :], big1[:, :], scal[:, SCL_ISG2:SCL_ISG2 + 1], MAGIC,
                ALU.mult, ALU.add), waits=[t_a1])
            t1 = vop(lambda v: v.tensor_scalar(
                big1[:, :], big1[:, :], MAGIC, scal[:, SCL_SG2:SCL_SG2 + 1],
                ALU.subtract, ALU.mult), waits=[t1])
            t2 = vop(lambda v: v.tensor_scalar(
                big2[:, :], big2[:, :], scal[:, SCL_ISUP:SCL_ISUP + 1], MAGIC,
                ALU.mult, ALU.add), waits=[t_a2])
            t2 = vop(lambda v: v.tensor_scalar(
                big2[:, :], big2[:, :], MAGIC, scal[:, SCL_SUP:SCL_SUP + 1],
                ALU.subtract, ALU.mult), waits=[t2])
            t3 = vop(lambda v: v.tensor_mul(big1[:, :], big1[:, :], big2[:, :]),
                     waits=[t1, t2])
            t_mr = vop(lambda v: v.reduce_max(runmax[:, RM_T:RM_T + 1], big1[:, :],
                                              axis=AXX, apply_absolute_value=True),
                       waits=[t3])
            if it == 0:
                vop(lambda v: v.tensor_copy(runmax[:, RM_O:RM_O + 1],
                                            runmax[:, RM_T:RM_T + 1]), waits=[t_mr])
            else:
                vop(lambda v: v.tensor_max(runmax[:, RM_O:RM_O + 1],
                                           runmax[:, RM_O:RM_O + 1],
                                           runmax[:, RM_T:RM_T + 1]), waits=[t_mr])
            dma("sync", k3(o_sp, NKF, t0, TB), sb3(big1, NKF), waits=[t3], sem=SPL)
            sc.barrier()
    if STOP_PHASE >= 7:
        do_cc(3, [(RM_O, SCL_SO, SCL_ISO)])

    if STOP_PHASE >= 8:
        ar.reset()
        wrow = ar.alloc("wrowE", [128, I], F32)
        wsc = ar.alloc("wscE", [128, 2 * (I // 32)], F32)
        wqb = ar.alloc("wqbE", [128, I], BF16)
        t_w = lpbq_load(wdown, NKD, I, wrow, wsc, wqb)
        sc.barrier()
        ar.reset()
        obig = ar.alloc("obig", [128, NKF * TB], F32)
        oqb = ar.alloc("oqb", [128, NKF * TB], BF16)
        gsil = ar.alloc("gsil", [128, NKD * TB], F32)
        sres = ar.alloc("sres", [128, NKD * TB], F32)
        first = True
        for it in range(ntB):
            t0 = it * TB
            t_in = dma("sync", sb3(obig, NKF), k3(o_sp, NKF, t0, TB), sem=LDA)
            t1 = vop(lambda v: v.tensor_scalar(
                obig[:, :], obig[:, :], scal[:, SCL_ISO:SCL_ISO + 1], MAGIC,
                ALU.mult, ALU.add), waits=[t_in])
            t_q = vop(lambda v: v.tensor_scalar(
                oqb[:, :], obig[:, :], MAGIC, scal[:, SCL_SO:SCL_SO + 1],
                ALU.subtract, ALU.mult), waits=[t1])
            t_sl = dma("sync", sb3(sres, NKD), k3(s_sp, NKD, t0, TB), sem=LDB)
            if DEBUG_SPILL:
                dma("sync", k3(oq_dbg, NKF, t0, TB), sb3(oqb, NKF),
                    waits=[t_q], sem=SPL)
                if it == 0:
                    dma("sync", wd_dbg, wsl(0, NKF * D), waits=[t_w], sem=SPL)
            ipev = {}
            for oc in range(NKD):
                pb = g1 if oc % 2 == 0 else g2p
                last = None
                for k in range(NKF):
                    wts = []
                    if k == 0:
                        wts = [t_q] + ([t_w] if first else [])
                        if oc >= 2:
                            wts.append(ipev[oc - 2])
                    last = mm(pb[:, 0:TB], wsl(k * D + oc * 128, 128),
                              oqb[:, k * TB:(k + 1) * TB],
                              start=(k == 0), stop=(k == NKF - 1),
                              waits=wts, inc=TN if k == NKF - 1 else None)
                first = False
                ipev[oc] = sop(lambda s, oc=oc, pb=pb: s.activation(
                    gsil[:, oc * TB:(oc + 1) * TB], pb[:, 0:TB], ACT.Silu),
                    waits=[last])
            t_g = ipev[NKD - 1]
            if DEBUG_SPILL:
                dma("sync", k3(g_dbg, NKD, t0, TB), sb3(gsil, NKD), waits=[t_g], sem=SPL)
            t_sg = vop(lambda v: v.tensor_mul(gsil[:, :], gsil[:, :], sres[:, :]),
                       waits=[t_g, t_sl])
            t_x2 = dma("sync", sb3(sres, NKD), k3(x_hbm, NKD, t0, TB),
                       waits=[t_sg], sem=LDC)
            if DEBUG_SPILL:
                dma("sync", k3(sg_dbg, NKD, t0, TB), sb3(gsil, NKD),
                    waits=[t_sg], sem=SPL)
            t_o = vop(lambda v: v.tensor_add(gsil[:, :], gsil[:, :], sres[:, :]),
                      waits=[t_x2])
            dma("sync", k3(out_hbm, NKD, t0, TB), sb3(gsil, NKD), waits=[t_o], sem=SPL)
            sc.barrier()

    sc.barrier()
    for e in Sched.ENGINES:
        sc.emit(e, lambda en: en.engine_nop() if hasattr(en, "engine_nop")
                else en.nop())

    for s in sc.sems:
        s.h = nc.alloc_semaphore(s.name)
    block = ctx.enter_context(nc.Block())

    def run(engine_name):
        def body(e):
            for fn in sc.ops[engine_name]:
                fn(e)
        return body

    block.sync(run("sync"))
    block.gpsimd(run("gpsimd"))
    block.tensor(run("tensor"))
    block.vector(run("vector"))
    block.scalar(run("scalar"))

    ctx.close()
    return nc


def _fm(x2d, nk):
    t = np.ascontiguousarray(x2d.T).reshape(nk, 128, -1)
    return np.ascontiguousarray(t.transpose(1, 0, 2).reshape(128, -1))


def _wstat(w, nk):
    out_d, in_d = w.shape
    t = np.ascontiguousarray(w.T).reshape(nk, 128, out_d)
    return np.ascontiguousarray(t.transpose(1, 0, 2).reshape(128, nk * out_d))


def kernel(L0_attn_X, L0_attn_in_proj_w, L0_attn_conv_w, L0_attn_conv_b,
           L0_attn_dt_bias, L0_attn_A_log, L0_attn_D, L0_attn_norm_w,
           L0_attn_out_proj_w, L0_attn_to_h_w, L0_ffn_w_gate, L0_ffn_w_up,
           L0_ffn_w_down):
    cfg = CFG
    nco = cfg.ncore
    X = np.asarray(L0_attn_X, dtype=np.float32)
    B, L, D = X.shape
    NKD, NKI, NKC, NKF = D // 128, cfg.DI // 128, cfg.CONV // 128, cfg.I // 128

    win = _wstat(np.asarray(L0_attn_in_proj_w, np.float32), NKD)

    def _wrow(w):
        o, i = w.shape
        return np.ascontiguousarray(
            w.reshape(o // 128, 128, i).transpose(1, 0, 2).reshape(128, -1))

    wup = _wrow(np.asarray(L0_ffn_w_up, np.float32))
    wgate = _wrow(np.asarray(L0_ffn_w_gate, np.float32))
    wdown = _wrow(np.asarray(L0_ffn_w_down, np.float32))
    woutp = _wstat(np.asarray(L0_attn_out_proj_w, np.float32), NKI)
    wtoh = _wstat(np.asarray(L0_attn_to_h_w, np.float32)[:D, :], NKD)
    convw = np.ascontiguousarray(
        np.asarray(L0_attn_conv_w, np.float32)[:, 0, :]
        .reshape(NKC, 128, 4).transpose(1, 0, 2).reshape(128, NKC * 4))
    convb = np.ascontiguousarray(
        np.asarray(L0_attn_conv_b, np.float32).reshape(NKC, 128).T)
    alog = np.ascontiguousarray(
        np.tile(np.asarray(L0_attn_A_log, np.float32), 4)[:, None])
    dtb = np.ascontiguousarray(
        np.tile(np.asarray(L0_attn_dt_bias, np.float32), 4)[:, None])
    Dv = np.asarray(L0_attn_D, np.float32)
    dvec = np.empty((128, NKI), np.float32)
    for c in range(NKI):
        dvec[0:64, c] = Dv[2 * c]
        dvec[64:128, c] = Dv[2 * c + 1]
    normw = np.ascontiguousarray(
        np.asarray(L0_attn_norm_w, np.float32).reshape(NKI, 128).T)

    nc = build_program(cfg)

    seq_per_core = B // nco
    in_maps = []
    for c in range(nco):
        xs = X[c * seq_per_core:(c + 1) * seq_per_core].reshape(-1, D)
        in_maps.append({
            "x_fm": _fm(xs, NKD), "win": win, "wup": wup, "wgate": wgate,
            "wdown": wdown, "woutp": woutp, "wtoh": wtoh, "convw": convw,
            "convb": convb, "alog": alog, "dvec": dvec, "dtb": dtb,
            "normw": normw,
        })
    res = run_bass_kernel_spmd(nc, in_maps, core_ids=list(range(nco)))
    outs = []
    for c in range(nco):
        o = res.results[c]["out_fm"]
        o = o.reshape(128, NKD, -1).transpose(1, 0, 2).reshape(D, -1).T
        outs.append(o.reshape(seq_per_core, L, D))
    return np.concatenate(outs, axis=0)


# version 07 (final)
# speedup vs baseline: 1.0140x; 1.0140x over previous
import math
import numpy as np
import concourse.bass as bass
import concourse.mybir as mybir
from concourse.bass_utils import run_bass_kernel_spmd

F32 = mybir.dt.float32
BF16 = mybir.dt.bfloat16
ALU = mybir.AluOpType
ACT = mybir.ActivationFunctionType
AXX = mybir.AxisListType.X
MAGIC = 12582912.0
QMAX = 32767.0
EPS = 1e-5


class Cfg:
    def __init__(self, nseq=4, L=2048, tileA=256, tileB=512, ncore=8):
        self.nseq, self.L, self.ncore = nseq, L, ncore
        self.tileA, self.tileB = tileA, tileB
        self.T = nseq * L
        assert L % tileA == 0 and L % tileB == 0
        assert tileA % 128 == 0 and tileB % 128 == 0
        self.D, self.DI, self.DS, self.NH, self.HD = 1024, 2048, 64, 32, 64
        self.CONV, self.DIP, self.I = 2176, 4256, 3072


CFG = Cfg()
DEBUG_SPILL = False
STOP_PHASE = 99
STOP_A1 = 99


class Sem:
    __slots__ = ("name", "n", "h")

    def __init__(self, name):
        self.name, self.n, self.h = name, 0, None


class Sched:
    ENGINES = ("sync", "gpsimd", "tensor", "vector", "scalar")

    def __init__(self):
        self.ops = {e: [] for e in self.ENGINES}
        self.sems = []
        self.pending = {e: [] for e in self.ENGINES}

    def sem(self, name):
        s = Sem(name)
        self.sems.append(s)
        return s

    def barrier(self):
        snap = [(s, s.n) for s in self.sems if s.n > 0]
        for e in self.ENGINES:
            self.pending[e] = list(snap)

    def emit(self, eng, fn, waits=(), inc=None, amt=1):
        waits = list(self.pending[eng]) + [(s, v) for (s, v) in waits
                                           if s is not None and v > 0]
        self.pending[eng] = []
        tok = None
        if inc is not None:
            inc.n += amt
            tok = (inc, inc.n)

        def g(e, waits=tuple(waits), fn=fn, inc=inc, amt=amt):
            for (s, v) in waits:
                e.wait_ge(s.h, v)
            r = fn(e)
            if inc is not None:
                r.then_inc(inc.h, amt)
        self.ops[eng].append(g)
        return tok


class Arena:
    def __init__(self, nc, base, size):
        self.nc, self.base, self.size = nc, base, size
        self.off = 0
        self.uid = 0

    def reset(self):
        self.off = 0

    def alloc(self, name, shape, dt):
        nb = int(np.prod(shape[1:])) * mybir.dt.size(dt)
        nb = (nb + 31) // 32 * 32
        assert self.off + nb <= self.size, \
            f"arena overflow: {name} needs {nb} at off {self.off} / {self.size}"
        self.uid += 1
        t = self.nc.alloc_sbuf_tensor_at(
            f"ar{self.uid}_{name}", shape, dt, offset=self.base + self.off)
        self.off += nb
        return t


def build_program(cfg):
    nc = bass.Bass("TRN2", target_bir_lowering=False, detect_race_conditions=False)
    sc = Sched()
    T, L, nseq, TA, TB = cfg.T, cfg.L, cfg.nseq, cfg.tileA, cfg.tileB
    D, DI, NH, CONV, DIP, I = cfg.D, cfg.DI, cfg.NH, cfg.CONV, cfg.DIP, cfg.I
    NKD, NKI, NKC, NKF = D // 128, DI // 128, CONV // 128, I // 128
    NOC = (DIP + 127) // 128
    NCH = TA // 128
    ntA, ntB = T // TA, T // TB
    tps = L // TA

    def din(name, shape, dt=F32):
        return nc.dram_tensor(name, shape, dt, kind="ExternalInput").ap()

    x_hbm = din("x_fm", [128, NKD * T])
    win = din("win", [128, NKD * DIP])
    wup = din("wup", [128, NKF * D])
    wgate = din("wgate", [128, NKF * D])
    wdown = din("wdown", [128, NKD * I])
    woutp = din("woutp", [128, NKI * D])
    wtoh = din("wtoh", [128, NKD * D])
    convw_h = din("convw", [128, NKC * 4])
    convb_h = din("convb", [128, NKC])
    alog_h = din("alog", [128, 1])
    dvec_h = din("dvec", [128, NKI])
    dtb_h = din("dtb", [128, 1])
    normw_h = din("normw", [128, NKI])

    out_hbm = nc.dram_tensor("out_fm", [128, NKD * T], F32,
                             kind="ExternalOutput").ap()

    spk = dict(kind="ExternalOutput") if DEBUG_SPILL else {}
    xn_sp = nc.dram_tensor("xn_sp", [128, NKD * T], BF16, **spk).ap()
    xq_sp = nc.dram_tensor("xq_sp", [128, NKD * T], BF16, **spk).ap()
    yn_sp = nc.dram_tensor("yn_sp", [128, NKI * T], BF16, **spk).ap()
    s_sp = nc.dram_tensor("s_sp", [128, NKD * T], F32, **spk).ap()
    up_sp = nc.dram_tensor("up_sp", [128, NKF * T], F32, **spk).ap()
    gate_sp = nc.dram_tensor("gate_sp", [128, NKF * T], F32, **spk).ap()
    g2_sp = nc.dram_tensor("g2_sp", [128, NKF * T], F32, **spk).ap()
    o_sp = nc.dram_tensor("o_sp", [128, NKF * T], F32, **spk).ap()
    g_dbg = nc.dram_tensor("g_dbg", [128, NKD * T], F32, **spk).ap()
    oq_dbg = nc.dram_tensor("oq_dbg", [128, NKF * T], BF16, **spk).ap()
    wd_dbg = nc.dram_tensor("wd_dbg", [128, NKF * D], BF16, **spk).ap()
    sg_dbg = nc.dram_tensor("sg_dbg", [128, NKD * T], F32, **spk).ap()
    cc_in = [nc.dram_tensor(f"cc_in{k}", [1, 2], F32).ap() for k in range(4)]
    cc_out = [nc.dram_tensor(f"cc_out{k}", [1, 2], F32).ap() for k in range(4)]

    import contextlib
    ctx = contextlib.ExitStack()

    def sbt(name, shape, dt=F32):
        return nc.alloc_sbuf_tensor(name, shape, dt)

    ones1 = sbt("ones1", [1, 128])
    ones128 = sbt("ones128", [128, 1])
    id128 = sbt("id128", [128, 128])
    rowsel = sbt("rowsel", [32, NH * 128])
    pairsel = sbt("pairsel", [32, NKI * 128])
    csel = sbt("csel", [128, 64], BF16)
    bsel = sbt("bsel", [128, 64], BF16)
    ma = sbt("ma", [128, 128])
    zeros32 = sbt("zeros32", [32, 128])
    avec = sbt("avec", [128, 1])
    dvec = sbt("dvec_s", [128, NKI])
    dtbv = sbt("dtbv", [128, 1])
    normw = sbt("normw_s", [128, NKI])
    convw = sbt("convw_s", [128, NKC * 4])
    convb = sbt("convb_s", [128, NKC])
    c_one = sbt("c_one", [128, 1])
    c_eps = sbt("c_eps", [128, 1])
    scal = sbt("scal", [128, 12])
    runmax = sbt("runmax", [128, 8])
    rowA = sbt("rowA", [1, 256])
    rowB = sbt("rowB", [1, 256])
    rowC = sbt("rowC", [1, 256])
    ccrow = sbt("ccrow", [1, 2])

    WPOOL_COLS = NKD * DIP
    wpool = sbt("wpool", [128, WPOOL_COLS], BF16)

    def wsl(c0, n):
        assert c0 + n <= WPOOL_COLS
        return wpool[:, c0:c0 + n]

    ARENA = 108 * 1024
    rr = nc.bump_sbuf(ARENA)
    assert rr is not None, "no sbuf space for arena"
    ar = Arena(nc, rr[0], ARENA)

    pA = nc.alloc_psum_tensor("pA", [128, 1024], F32)
    pB = nc.alloc_psum_tensor("pB", [128, 1024], F32)
    pC = nc.alloc_psum_tensor("pC", [128, 1024], F32)
    g1 = nc.alloc_psum_tensor("g1", [128, 512], F32)
    g2p = nc.alloc_psum_tensor("g2p", [128, 512], F32)

    LD = sc.sem("LD")
    LDA = sc.sem("LDA")
    LDB = sc.sem("LDB")
    LDC = sc.sem("LDC")
    SPL = sc.sem("SPL")
    STG = sc.sem("STG")
    GD = sc.sem("GD")
    SD = sc.sem("SD")
    TN = sc.sem("TN")
    VC = sc.sem("VC")
    SCA = sc.sem("SCA")
    GP = sc.sem("GP")

    (SCL_IS0, SCL_S0, SCL_ISG, SCL_SG, SCL_ISUP, SCL_SUP,
     SCL_ISG2, SCL_SG2, SCL_ISO, SCL_SO) = range(10)
    RM_X, RM_UP, RM_G, RM_G2, RM_O, RM_T = range(6)

    def mm(out_ap, stat_ap, mov_ap, start, stop, waits=(), inc=None):
        return sc.emit(
            "tensor",
            lambda t, o=out_ap, s=stat_ap, m=mov_ap, a=start, b=stop:
                t.matmul(o, s, m, start=a, stop=b),
            waits=waits, inc=inc)

    def tr(out_ap, in_ap, idn, waits=(), inc=None):
        return sc.emit(
            "tensor",
            lambda t, o=out_ap, i=in_ap, d=idn: t.transpose(o, i, d),
            waits=waits, inc=inc)

    def vop(fn, waits=(), inc=None):
        return sc.emit("vector", fn, waits=waits, inc=inc or VC)

    def sop(fn, waits=(), inc=None):
        return sc.emit("scalar", fn, waits=waits, inc=inc or SCA)

    def gop(fn, waits=(), inc=None):
        return sc.emit("gpsimd", fn, waits=waits, inc=inc)

    def dma(eng, out_ap, in_ap, waits=(), sem=None):
        if sem is None:
            sem = {"sync": LD, "gpsimd": GD, "scalar": SD}[eng]
        return sc.emit(
            eng, lambda e, o=out_ap, i=in_ap: e.dma_start(out=o, in_=i),
            waits=waits, inc=sem, amt=16)

    def dmat(out_ap, in_ap, waits=()):
        return sc.emit(
            "sync",
            lambda e, o=out_ap, i=in_ap: e.dma_start(out=o, in_=i, transpose=True),
            waits=waits, inc=SD, amt=16)

    def k3(hbm_ap, nk, t0, n):
        return hbm_ap.rearrange("p (k t) -> p k t", k=nk)[:, :, t0:t0 + n]

    def sb3(t, nk):
        return t[:, :].rearrange("p (k t) -> p k t", k=nk)

    gop(lambda g: g.memset(ones1[:, :], 1.0))
    gop(lambda g: g.memset(ones128[:, :], 1.0))
    gop(lambda g: g.memset(zeros32[:, :], 0.0))
    gop(lambda g: g.memset(c_one[:, :], 1.0))
    gop(lambda g: g.memset(c_eps[:, :], EPS))
    gop(lambda g: g.memset(ccrow[:, :], 0.0))
    gop(lambda g: g.memset(runmax[:, :], 0.0))
    gop(lambda g: g.memset(scal[:, :], 1.0))
    gop(lambda g: g.memset(id128[:, :], 1.0))
    gop(lambda g: g.affine_select(
        out=id128[:, :], in_=id128[:, :], compare_op=ALU.is_equal,
        fill=0.0, base=0, pattern=[[-1, 128]], channel_multiplier=1))
    gop(lambda g: g.memset(ma[:, :], 0.0))
    gop(lambda g: g.affine_select(
        out=ma[:, :], in_=ma[:, :], compare_op=ALU.is_ge,
        fill=-1e30, base=0, pattern=[[1, 128]], channel_multiplier=-1))
    gop(lambda g: g.memset(rowsel[:, :], 1.0))
    gop(lambda g: g.affine_select(
        out=rowsel[:, :].rearrange("p (h t) -> p h t", h=NH),
        in_=rowsel[:, :].rearrange("p (h t) -> p h t", h=NH),
        compare_op=ALU.is_equal, fill=0.0, base=0,
        pattern=[[-1, NH], [0, 128]], channel_multiplier=1))
    gop(lambda g: g.memset(pairsel[:, :], 1.0))
    gop(lambda g: g.affine_select(
        out=pairsel[:, :].rearrange("p (c u q) -> p c u q", c=NKI, u=2),
        in_=pairsel[:, :].rearrange("p (c u q) -> p c u q", c=NKI, u=2),
        compare_op=ALU.is_equal, fill=0.0, base=0,
        pattern=[[-2, NKI], [-1, 2], [0, 64]], channel_multiplier=1))
    gop(lambda g: g.memset(csel[:, :], 0.0))
    gop(lambda g: g.memset(csel[64:128, :], 1.0))
    gop(lambda g: g.affine_select(
        out=csel[64:128, :], in_=csel[64:128, :], compare_op=ALU.is_equal,
        fill=0.0, base=0, pattern=[[-1, 64]], channel_multiplier=1))
    gop(lambda g: g.memset(bsel[:, :], 0.0))
    gop(lambda g: g.memset(bsel[0:64, :], 1.0))
    gop(lambda g: g.affine_select(
        out=bsel[0:64, :], in_=bsel[0:64, :], compare_op=ALU.is_equal,
        fill=0.0, base=0, pattern=[[-1, 64]], channel_multiplier=1), inc=GP)

    for (dst, src) in ((avec, alog_h), (dvec, dvec_h), (dtbv, dtb_h),
                       (normw, normw_h), (convw, convw_h), (convb, convb_h)):
        dma("sync", dst[:, :], src)
    t_win = dma("gpsimd", wsl(0, NKD * DIP), win)
    sc.barrier()
    t_av = sop(lambda s: s.activation(avec[:, :], avec[:, :], ACT.Exp))
    vop(lambda v: v.tensor_scalar(avec[:, :], avec[:, :], -1.0, None, ALU.mult),
        waits=[t_av])
    sc.barrier()

    ar.reset()
    xt = ar.alloc("xt", [128, NKD * TA], F32)
    xnb = ar.alloc("xnb", [128, NKD * TA], BF16)
    tmpa = ar.alloc("tmpa", [128, 2 * TA], F32)
    zsil = ar.alloc("zsil", [128, NKI * TA], BF16)
    xbcr = ar.alloc("xbcr", [128, NKC * (TA + 4)], F32)
    xbcp = ar.alloc("xbcp", [128, NKC * TA], BF16)
    xtok = ar.alloc("xtok", [128, NKI * 128], BF16)
    mtmp = ar.alloc("mtmp", [128, 16 * 128], F32)
    msb = ar.alloc("msb", [128, 16 * 128], BF16)
    csb = ar.alloc("csb", [64, TA], F32)
    bsb = ar.alloc("bsb", [128, TA], F32)
    cbsb = ar.alloc("cbsb", [128, 128], F32)
    btok = ar.alloc("btok", [128, 64], BF16)
    dtsb = ar.alloc("dtsb", [128, TA], F32)
    ssb = ar.alloc("ssb", [128, TA], F32)
    usb = ar.alloc("usb", [128, TA], F32)
    esb = ar.alloc("esb", [128, 128], F32)
    lam = ar.alloc("lam", [128, 128], F32)
    ut_t = ar.alloc("ut", [128, 32], F32)
    lt_t = ar.alloc("lt", [128, 32], F32)
    da_row = ar.alloc("da", [1, 32], F32)
    lamx = ar.alloc("lamx", [128, NKI * 128], BF16)
    hsb = ar.alloc("hsb", [64, DI], F32)
    ysb = ar.alloc("ysb", [128, NKI * 128], F32)
    ytmp = ar.alloc("ytmp", [128, 8 * 128], F32)
    ynb = ar.alloc("ynb", [128, NKI * TA], BF16)

    for it in range(ntA):
        t0 = it * TA
        seq_start = (it % tps == 0)

        t_x = dma("sync", sb3(xt, NKD), k3(x_hbm, NKD, t0, TA), sem=LDA)
        if it == 0:
            for buf in (dtsb, ssb, usb, bsb):
                vop(lambda v, b=buf: v.memset(b[:, :], 0.0))
            vop(lambda v: v.memset(esb[:, :], 0.0))
            vop(lambda v: v.memset(lam[:, :], 0.0))

        t_s1 = t_s2 = None
        sqmm = {}
        for k in range(NKD):
            slot = k % 2
            wsq = [t_x]
            if k >= 2:
                wsq.append(sqmm[k - 2])
            t_q = sop(lambda s, k=k, slot=slot: s.activation(
                tmpa[:, slot * TA:(slot + 1) * TA],
                xt[:, k * TA:(k + 1) * TA], ACT.Square), waits=wsq)
            t_s2 = mm(pB[0:1, 0:TA], ones128[:, :], tmpa[:, slot * TA:(slot + 1) * TA],
                      start=(k == 0), stop=(k == NKD - 1), waits=[t_q],
                      inc=TN)
            sqmm[k] = t_s2
            t_s1 = mm(pA[0:1, 0:TA], ones128[:, :], xt[:, k * TA:(k + 1) * TA],
                      start=(k == 0), stop=(k == NKD - 1),
                      waits=[t_x] if k == 0 else (),
                      inc=TN if k == NKD - 1 else None)
        t_m = vop(lambda v: v.tensor_scalar(rowA[0:1, 0:TA], pA[0:1, 0:TA],
                                            1.0 / D, None, ALU.mult), waits=[t_s1])
        t_m2 = vop(lambda v: v.tensor_mul(rowB[0:1, 0:TA], rowA[0:1, 0:TA],
                                          rowA[0:1, 0:TA]), waits=[t_m])
        t_var = vop(lambda v: v.scalar_tensor_tensor(
            rowC[0:1, 0:TA], pB[0:1, 0:TA], 1.0 / D, rowB[0:1, 0:TA],
            ALU.mult, ALU.subtract), waits=[t_s2, t_m2])
        t_sd = sop(lambda s: s.activation(rowC[0:1, 0:TA], rowC[0:1, 0:TA],
                                          ACT.Sqrt, bias=c_eps[0:1, :]),
                   waits=[t_var])
        t_rstd = vop(lambda v: v.reciprocal(rowC[0:1, 0:TA], rowC[0:1, 0:TA]),
                     waits=[t_sd])
        t_mb = vop(lambda v: v.scalar_tensor_tensor(
            rowB[0:1, 0:TA], rowA[0:1, 0:TA], -1.0, rowC[0:1, 0:TA],
            ALU.mult, ALU.mult), waits=[t_rstd])
        t_bs = mm(g1[:, 0:TA], ones1[:, :], rowC[0:1, 0:TA], True, True,
                  waits=[t_rstd], inc=TN)
        t_bb = mm(g2p[:, 0:TA], ones1[:, :], rowB[0:1, 0:TA], True, True,
                  waits=[t_mb], inc=TN)
        t_xn = None
        for k in range(NKD):
            vop(lambda v, k=k: v.tensor_mul(
                xt[:, k * TA:(k + 1) * TA], xt[:, k * TA:(k + 1) * TA],
                g1[:, 0:TA]), waits=[t_bs, t_bb] if k == 0 else ())
            t_xn = vop(lambda v, k=k: v.tensor_add(
                xnb[:, k * TA:(k + 1) * TA], xt[:, k * TA:(k + 1) * TA],
                g2p[:, 0:TA]))
        t_r = vop(lambda v: v.reduce_max(runmax[:, RM_T:RM_T + 1], xnb[:, :],
                                         axis=AXX, apply_absolute_value=True),
                  waits=[t_xn])
        if it == 0:
            vop(lambda v: v.tensor_copy(runmax[:, RM_X:RM_X + 1],
                                        runmax[:, RM_T:RM_T + 1]), waits=[t_r])
        else:
            vop(lambda v: v.tensor_max(runmax[:, RM_X:RM_X + 1],
                                       runmax[:, RM_X:RM_X + 1],
                                       runmax[:, RM_T:RM_T + 1]), waits=[t_r])
        dma("sync", k3(xn_sp, NKD, t0, TA), sb3(xnb, NKD), waits=[t_xn], sem=SPL)

        if STOP_A1 < 2:
            continue
        ipev = {}
        t_dt = None
        slots = [g1[:, 0:TA], g2p[:, 0:TA], pA[:, 0:TA], pA[:, 512:512 + TA],
                 pB[:, 0:TA], pB[:, 512:512 + TA]]
        for oc in range(NOC):
            ow = min(128, DIP - oc * 128)
            pb = slots[oc % 6]
            last = None
            for k in range(NKD):
                wts = []
                if k == 0:
                    wts = [t_xn]
                    if oc >= 6:
                        wts.append(ipev[oc - 6])
                last = mm(pb[0:ow, :], wsl(k * DIP + oc * 128, ow),
                          xnb[:, k * TA:(k + 1) * TA],
                          start=(k == 0), stop=(k == NKD - 1),
                          waits=wts, inc=TN if k == NKD - 1 else None)
            if oc < NKI:
                ev = sop(lambda s, oc=oc, pb=pb: s.activation(
                    zsil[:, oc * TA:(oc + 1) * TA], pb[:, :], ACT.Silu),
                    waits=[last])
            elif oc < NKI + NKC:
                c = oc - NKI
                ev = vop(lambda v, c=c, pb=pb: v.tensor_copy(
                    xbcr[:, c * (TA + 4) + 4: c * (TA + 4) + 4 + TA],
                    pb[:, 0:TA]), waits=[last])
            else:
                ev = sop(lambda s, pb=pb: s.activation(
                    dtsb[0:32, 0:TA], pb[0:32, :], ACT.Exp,
                    bias=dtbv[0:32, :]), waits=[last])
                t_dt = sop(lambda s: s.activation(
                    dtsb[0:32, 0:TA], dtsb[0:32, 0:TA], ACT.Ln,
                    bias=c_one[0:32, :]), waits=[ev])
                ev = t_dt
            ipev[oc] = ev

        if STOP_A1 < 3:
            continue
        if seq_start:
            t_halo = vop(lambda v: v.memset(
                xbcr[:, :].rearrange("p (c t) -> p c t", c=NKC)[:, :, 0:4], 0.0))
        else:
            t_halo = None
        convk = {}
        for c in range(NKC):
            base = c * (TA + 4)
            slot = c % 2
            wcv = [ipev[NKI + c]]
            if t_halo is not None:
                wcv.append(t_halo)
            if c >= 2:
                wcv.append(convk[c - 2])
            tt = vop(lambda v, base=base, c=c, slot=slot: v.tensor_scalar(
                xt[:, slot * TA:(slot + 1) * TA],
                xbcr[:, base + 1: base + 1 + TA],
                convw[:, 4 * c:4 * c + 1], None, ALU.mult), waits=wcv)
            for tap in (1, 2, 3):
                tt = vop(lambda v, base=base, c=c, tap=tap, slot=slot:
                         v.scalar_tensor_tensor(
                             xt[:, slot * TA:(slot + 1) * TA],
                             xbcr[:, base + 1 + tap: base + 1 + tap + TA],
                             convw[:, 4 * c + tap:4 * c + tap + 1],
                             xt[:, slot * TA:(slot + 1) * TA],
                             ALU.mult, ALU.add))
            convk[c] = sop(lambda s, c=c, slot=slot: s.activation(
                xbcp[:, c * TA:(c + 1) * TA],
                xt[:, slot * TA:(slot + 1) * TA], ACT.Silu,
                bias=convb[:, c:c + 1]), waits=[tt])
        t_halo_n = vop(lambda v: v.tensor_copy(
            xbcr[:, :].rearrange("p (c t) -> p c t", c=NKC)[:, :, 0:4],
            xbcr[:, :].rearrange("p (c t) -> p c t", c=NKC)[:, :, TA:TA + 4]),
            waits=[convk[NKC - 1]])

        if STOP_A1 < 4:
            continue
        t_a = vop(lambda v: v.tensor_scalar(
            ssb[0:32, 0:TA], dtsb[0:32, 0:TA], avec[0:32, 0:1], None, ALU.mult),
            waits=[t_dt])
        t_s = None
        for c in range(NCH):
            t_s = vop(lambda v, c=c: v.tensor_tensor_scan(
                ssb[0:32, c * 128:(c + 1) * 128],
                ssb[0:32, c * 128:(c + 1) * 128],
                zeros32[:, :], 0.0, ALU.add, ALU.add),
                waits=[t_a] if c == 0 else ())
        t_lndt = sop(lambda s: s.activation(usb[0:32, 0:TA], dtsb[0:32, 0:TA],
                                            ACT.Ln), waits=[t_dt])
        t_u = vop(lambda v: v.tensor_sub(usb[0:32, 0:TA], ssb[0:32, 0:TA],
                                         usb[0:32, 0:TA]), waits=[t_lndt, t_s])

        if STOP_A1 < 5:
            continue
        t_ce = mm(pC[0:64, 0:TA], csel[:, :], xbcp[:, 16 * TA:17 * TA],
                  True, True, waits=[convk[16]], inc=TN)
        t_cs = vop(lambda v: v.tensor_copy(csb[:, 0:TA], pC[0:64, 0:TA]),
                   waits=[t_ce])
        t_be = mm(pC[0:64, 0:TA], bsel[:, :], xbcp[:, 16 * TA:17 * TA],
                  True, True, waits=[t_cs], inc=TN)
        t_bs2 = vop(lambda v: v.tensor_copy(bsb[0:64, 0:TA], pC[0:64, 0:TA]),
                    waits=[t_be])

        if seq_start:
            t_h0 = vop(lambda v: v.memset(hsb[:, :], 0.0))
        else:
            t_h0 = None

        for cch in range(NCH):
            tc0 = cch * 128
            sc.barrier()
            if STOP_A1 < 6:
                continue
            t_xtk = None
            for fc in range(NKI):
                t_xtk = dmat(xtok[:, fc * 128:(fc + 1) * 128],
                             xbcp[:, fc * TA + tc0: fc * TA + tc0 + 128],
                             waits=[convk[fc]] if fc == 0 else ())
            if STOP_A1 < 7:
                continue
            t_ut = tr(g1[0:128, 0:128], usb[:, tc0:tc0 + 128],
                      id128[:, :], waits=[t_u], inc=TN)
            t_le = vop(lambda v, tc0=tc0: v.tensor_scalar(
                lam[0:32, :], ssb[0:32, tc0:tc0 + 128],
                ssb[0:32, tc0 + 127:tc0 + 128], None, ALU.subtract), waits=[t_s])
            t_le = sop(lambda s: s.activation(lam[0:32, :], lam[0:32, :], ACT.Exp,
                                              scale=-1.0), waits=[t_le])
            t_le = vop(lambda v, tc0=tc0: v.tensor_mul(
                lam[0:32, :], lam[0:32, :], dtsb[0:32, tc0:tc0 + 128]),
                waits=[t_le])
            t_ee = sop(lambda s, tc0=tc0: s.activation(
                esb[0:32, :], ssb[0:32, tc0:tc0 + 128], ACT.Exp), waits=[t_s])
            t_lt = tr(g1[0:128, 128:256], lam[:, :], id128[:, :],
                      waits=[t_le], inc=TN)
            t_cb = mm(g1[0:128, 256:384], bsb[0:64, tc0:tc0 + 128],
                      csb[:, tc0:tc0 + 128], True, True, waits=[t_bs2], inc=TN)
            t_da = tr(g2p[0:1, 256:384], esb[:, 127:128], id128[:, :],
                      waits=[t_ee], inc=TN)
            t_bt = tr(g2p[0:128, 128:256], bsb[:, tc0:tc0 + 128],
                      id128[:, :], waits=[t_bs2], inc=TN)
            t_ute = vop(lambda v: v.tensor_copy(ut_t[:, :], g1[0:128, 0:32]),
                        waits=[t_cb])
            t_lte = vop(lambda v: v.tensor_copy(lt_t[:, :], g1[0:128, 128:160]))
            t_cbe = vop(lambda v: v.tensor_copy(cbsb[:, :], g1[0:128, 256:384]))
            t_dae = vop(lambda v: v.tensor_copy(da_row[:, :], g2p[0:1, 256:288]),
                        waits=[t_bt])
            t_bte = vop(lambda v: v.tensor_copy(btok[:, :], g2p[0:128, 128:192]))
            t_lx = vop(lambda v: v.tensor_tensor(
                lamx[:, :], xtok[:, :],
                lt_t[:, :].unsqueeze(2).broadcast_to([128, 32, 64]), ALU.mult),
                waits=[t_lte, t_xtk])

            if STOP_A1 < 8:
                continue
            t_yb_all = []
            t_half = None
            stp_prev = {}
            m_prev = None
            ya_prev = None
            for half in range(2):
                h0, f0 = half * 16, half * 8
                t_st1 = None
                for hh in range(16):
                    h = h0 + hh
                    slot = hh % 2
                    sl_ap = (g1[0:128, 384:512] if slot == 0 else g2p[0:128, 0:128])
                    wst = [t_s]
                    if hh <= 1 and half == 0:
                        wst += [t_ute, t_lte, t_cbe, t_dae, t_bte]
                    if slot in stp_prev:
                        wst.append(stp_prev[slot])
                    t_sp = mm(sl_ap, rowsel[:, h * 128:(h + 1) * 128],
                              ssb[0:32, tc0:tc0 + 128], True, True,
                              waits=wst, inc=TN)
                    wst1 = [t_sp, t_ute]
                    if hh == 0 and m_prev is not None:
                        wst1.append(m_prev)
                    t_st1 = vop(lambda v, hh=hh, sl_ap=sl_ap, h=h:
                                v.scalar_tensor_tensor(
                                    mtmp[:, hh * 128:(hh + 1) * 128], sl_ap,
                                    ut_t[:, h:h + 1], ma[:, :],
                                    ALU.subtract, ALU.add), waits=wst1)
                    stp_prev[slot] = t_st1
                t_exp = sop(lambda s: s.activation(mtmp[:, :], mtmp[:, :],
                                                   ACT.Exp), waits=[t_st1])
                wm = [t_exp, t_cbe]
                if ya_prev is not None:
                    wm.append(ya_prev)
                t_m = vop(lambda v: v.tensor_tensor(
                    msb[:, :], mtmp[:, :],
                    cbsb[:, :].unsqueeze(1).broadcast_to([128, 16, 128]),
                    ALU.mult), waits=wm)
                m_prev = t_m

                if STOP_A1 < 9:
                    continue
                wtrio = [t_half] if t_half is not None else []
                t_ya = None
                for hh in range(16):
                    h = h0 + hh
                    fm, bp = hh // 2, 64 * (hh % 2)
                    t_ya = mm(pA[bp:bp + 64, fm * 128:(fm + 1) * 128],
                              xtok[:, h * 64:(h + 1) * 64],
                              msb[:, hh * 128:(hh + 1) * 128], True, True,
                              waits=[t_m] + ([t_xtk] + wtrio if hh == 0 else []),
                              inc=TN)
                ya_prev = t_ya
                t_yb = None
                for fm in range(8):
                    fc = f0 + fm
                    wyb = []
                    if fm == 0:
                        wyb = [t_cs] + wtrio
                        if t_h0 is not None and cch == 0:
                            wyb.append(t_h0)
                    t_yb = mm(pB[:, fm * 128:(fm + 1) * 128],
                              hsb[:, fc * 128:(fc + 1) * 128],
                              csb[:, tc0:tc0 + 128], True, True,
                              waits=wyb, inc=TN)
                t_yb_all.append(t_yb)
                t_yc = None
                for fm in range(8):
                    fc = f0 + fm
                    t_yc = mm(pC[:, fm * 128:(fm + 1) * 128],
                              pairsel[:, fc * 128:(fc + 1) * 128],
                              esb[0:32, :], True, True,
                              waits=([t_ee] + wtrio) if fm == 0 else (), inc=TN)

                if STOP_A1 < 10:
                    continue
                t_c0 = vop(lambda v, f0=f0: v.tensor_copy(
                    ysb[:, f0 * 128:(f0 + 8) * 128], pC[:, 0:1024]),
                    waits=[t_yc])
                t_c1 = vop(lambda v, f0=f0: v.tensor_mul(
                    ytmp[:, :], pB[:, 0:1024],
                    ysb[:, f0 * 128:(f0 + 8) * 128]), waits=[t_yb, t_c0])
                t_c2 = vop(lambda v, f0=f0: v.tensor_add(
                    ysb[:, f0 * 128:(f0 + 8) * 128], ytmp[:, :], pA[:, 0:1024]),
                    waits=[t_ya, t_c1])
                t_c3 = vop(lambda v, f0=f0, tc0=tc0: v.tensor_tensor(
                    ytmp[:, :],
                    xbcp[:, :].rearrange("p (c t) -> p c t", c=NKC)[:, f0:f0 + 8, tc0:tc0 + 128],
                    dvec[:, f0:f0 + 8].unsqueeze(2).broadcast_to([128, 8, 128]),
                    ALU.mult), waits=[t_c2])
                t_c4 = vop(lambda v, f0=f0: v.tensor_add(
                    ysb[:, f0 * 128:(f0 + 8) * 128],
                    ysb[:, f0 * 128:(f0 + 8) * 128], ytmp[:, :]))
                t_half = vop(lambda v, f0=f0, tc0=tc0: v.tensor_mul(
                    ysb[:, f0 * 128:(f0 + 8) * 128],
                    ysb[:, f0 * 128:(f0 + 8) * 128],
                    zsil[:, :].rearrange("p (c t) -> p c t", c=NKI)[:, f0:f0 + 8, tc0:tc0 + 128]))

            if STOP_A1 < 11:
                continue
            t_hu = []
            for hf in range(2):
                pd = pA if hf == 0 else pB
                t_um = None
                for q in range(2):
                    t_um = mm(pd[0:64, q * 512:(q + 1) * 512], btok[:, :],
                              lamx[:, (hf * 2 + q) * 512:(hf * 2 + q + 1) * 512],
                              True, True,
                              waits=[t_bte, t_lx, t_half] if q == 0 else (),
                              inc=TN)
                t_hu.append(t_um)
            t_hs1 = None
            for hf in range(2):
                pd = pA if hf == 0 else pB
                for qq in range(2):
                    t_db = mm(pC[0:64, qq * 512:(qq + 1) * 512],
                              ones1[0:1, 0:64],
                              da_row[0:1, hf * 16 + qq * 8: hf * 16 + (qq + 1) * 8]
                              .unsqueeze(2).broadcast_to([1, 8, 64]),
                              True, True,
                              waits=([t_dae, t_half] if hf == 0 else [t_hs0])
                              if qq == 0 else (), inc=TN)
                t_hs0 = vop(lambda v, hf=hf: v.tensor_mul(
                    hsb[:, hf * 1024:(hf + 1) * 1024],
                    hsb[:, hf * 1024:(hf + 1) * 1024], pC[0:64, 0:1024]),
                    waits=[t_db, t_yb_all[0], t_yb_all[1]])
                t_hs1 = vop(lambda v, hf=hf: v.tensor_add(
                    hsb[:, hf * 1024:(hf + 1) * 1024],
                    hsb[:, hf * 1024:(hf + 1) * 1024], pd[0:64, 0:1024]),
                    waits=[t_hu[hf]])

            if STOP_A1 < 12:
                continue
            t_q2 = sop(lambda s: s.activation(mtmp[:, :], ysb[:, 0:2048], ACT.Square),
                       waits=[t_half, t_hs1])
            t_r2 = None
            for k2 in range(NKI):
                t_r2 = mm(g2p[0:1, 128:128 + 128], ones128[:, :],
                          mtmp[:, k2 * 128:(k2 + 1) * 128],
                          start=(k2 == 0), stop=(k2 == NKI - 1),
                          waits=[t_q2] if k2 == 0 else (),
                          inc=TN if k2 == NKI - 1 else None)
            t_rs = vop(lambda v: v.tensor_scalar(
                rowA[0:1, 0:128], g2p[0:1, 128:256], 1.0 / DI, None, ALU.mult),
                waits=[t_r2])
            t_rs = sop(lambda s: s.activation(rowA[0:1, 0:128], rowA[0:1, 0:128],
                                              ACT.Sqrt, bias=c_eps[0:1, :]),
                       waits=[t_rs])
            t_rs = vop(lambda v: v.reciprocal(rowA[0:1, 0:128], rowA[0:1, 0:128]),
                       waits=[t_rs])
            t_rb = mm(g2p[:, 256:384], ones1[:, :], rowA[0:1, 0:128], True, True,
                      waits=[t_rs], inc=TN)
            t_yn = vop(lambda v, tc0=tc0: v.scalar_tensor_tensor(
                ysb[:, 0:2048], ysb[:, 0:2048], 1.0,
                g2p[:, 256:384].unsqueeze(1).broadcast_to([128, 16, 128]),
                ALU.mult, ALU.mult), waits=[t_rb])
            t_yn = vop(lambda v, tc0=tc0: v.tensor_tensor(
                ynb[:, :].rearrange("p (c t) -> p c t", c=NKI)[:, :, tc0:tc0 + 128],
                ysb[:, :].rearrange("p (c t) -> p c t", c=NKI)[:, :, 0:128],
                normw[:, :].unsqueeze(2).broadcast_to([128, NKI, 128]),
                ALU.mult), waits=[t_yn])
            t_yn_last = t_yn
        if STOP_A1 >= 12:
            dma("sync", k3(yn_sp, NKI, t0, TA), sb3(ynb, NKI),
                waits=[t_yn_last], sem=SPL)
        sc.barrier()

    def do_cc(k, pairs):
        sc.barrier()
        tt = None
        for i, (rm, _, _) in enumerate(pairs):
            tt = tr(g1[0:1, 0:128], runmax[:, rm:rm + 1], id128[:, :],
                    waits=[tt] if tt else (), inc=TN)
            tt = vop(lambda v, i=i: v.reduce_max(
                ccrow[0:1, i:i + 1], g1[0:1, 0:128], axis=AXX), waits=[tt])
        t_d = dma("gpsimd", cc_in[k], ccrow[0:1, 0:2], waits=[tt])
        t_c = gop(lambda g, k=k: g.collective_compute(
            "AllReduce", ALU.max, replica_groups=[list(range(cfg.ncore))],
            ins=[cc_in[k].opt()], outs=[cc_out[k].opt()]),
            waits=[t_d], inc=GP)
        t_b = dma("gpsimd", ccrow[0:1, 0:2], cc_out[k], waits=[t_c])
        tt = vop(lambda v: v.tensor_scalar(
            ccrow[0:1, 0:2], ccrow[0:1, 0:2], 1.0 / QMAX, 1e-12,
            ALU.mult, ALU.max), waits=[t_b])
        tt = mm(g1[:, 0:2], ones1[:, :], ccrow[0:1, 0:2], True, True,
                waits=[tt], inc=TN)
        toks = []
        for i, (_, s_col, i_col) in enumerate(pairs):
            t1 = vop(lambda v, i=i, s_col=s_col: v.tensor_copy(
                scal[:, s_col:s_col + 1], g1[:, i:i + 1]), waits=[tt])
            t2 = vop(lambda v, i=i, i_col=i_col: v.reciprocal(
                scal[:, i_col:i_col + 1], g1[:, i:i + 1]), waits=[tt])
            toks += [t1, t2]
        sc.barrier()
        return toks

    if STOP_PHASE >= 2:
        do_cc(0, [(RM_X, SCL_S0, SCL_IS0)])

    if STOP_PHASE >= 3:
        ar.reset()
        ynb2 = ar.alloc("ynb2", [128, NKI * TB], BF16)
        hbf = ar.alloc("hbf", [128, NKD * TB], BF16)
        sbf = ar.alloc("sbf", [128, NKD * TB], F32)
        t_w2 = dma("gpsimd", wsl(0, NKI * D), woutp)
        t_w3 = dma("gpsimd", wsl(NKI * D, NKD * D), wtoh)
        for it in range(ntB):
            t0 = it * TB
            slots6 = [g1[:, 0:TB], g2p[:, 0:TB], pA[:, 0:TB], pA[:, 512:512 + TB],
                  pB[:, 0:TB], pB[:, 512:512 + TB]]
            t_y = dma("sync", sb3(ynb2, NKI), k3(yn_sp, NKI, t0, TB), sem=LDA)
            ipev = {}
            for oc in range(NKD):
                pb = slots6[oc % 6]
                last = None
                for k in range(NKI):
                    wts = []
                    if k == 0:
                        wts = [t_y, t_w3]
                        if oc >= 6:
                            wts.append(ipev[oc - 6])
                    last = mm(pb[:, 0:TB], wsl(k * D + oc * 128, 128),
                              ynb2[:, k * TB:(k + 1) * TB],
                              start=(k == 0), stop=(k == NKI - 1),
                              waits=wts, inc=TN if k == NKI - 1 else None)
                ipev[oc] = sop(lambda s, oc=oc, pb=pb: s.activation(
                    hbf[:, oc * TB:(oc + 1) * TB], pb[:, :], ACT.Identity),
                    waits=[last])
            t_hb = ipev[NKD - 1]
            spev = {}
            for oc in range(NKD):
                pb = slots6[oc % 6]
                last = None
                for k in range(NKD):
                    wts = []
                    if k == 0:
                        wts = [t_hb, t_w3] + [ipev[j] for j in range(NKD - 5, NKD - 1)]
                        if oc >= 6:
                            wts.append(spev[oc - 6])
                    last = mm(pb[:, 0:TB], wsl(NKI * D + k * D + oc * 128, 128),
                              hbf[:, k * TB:(k + 1) * TB],
                              start=(k == 0), stop=(k == NKD - 1),
                              waits=wts, inc=TN if k == NKD - 1 else None)
                spev[oc] = vop(lambda v, oc=oc, pb=pb: v.tensor_copy(
                    sbf[:, oc * TB:(oc + 1) * TB], pb[:, :]), waits=[last])
            dma("sync", k3(s_sp, NKD, t0, TB), sb3(sbf, NKD), waits=[spev[NKD - 1]], sem=SPL)
            sc.barrier()

    def lpbq_load(w_hbm, nO, nI, wrow, wsc, wqb):
        nblk = nI // 32
        O = nO * 128
        t_tr = None
        prev_tr = None
        for oc in range(nO):
            wl = [prev_tr] if prev_tr is not None else []
            t_l = dma("sync", wrow[:, :],
                      w_hbm.rearrange("p (c i) -> p c i", c=nO)[:, oc, :],
                      waits=wl, sem=LDB)
            t1 = vop(lambda v: v.reduce_max(
                wsc[:, 0:nblk],
                wrow[:, :].rearrange("p (b q) -> p b q", q=32),
                axis=AXX, apply_absolute_value=True), waits=[t_l])
            t1 = vop(lambda v: v.tensor_scalar(
                wsc[:, 0:nblk], wsc[:, 0:nblk], 1.0 / 7.0, 1e-12,
                ALU.mult, ALU.max), waits=[t1])
            t1 = vop(lambda v: v.reciprocal(wsc[:, nblk:2 * nblk], wsc[:, 0:nblk]),
                     waits=[t1])
            t2 = vop(lambda v: v.tensor_tensor(
                wrow[:, :], wrow[:, :],
                wsc[:, nblk:2 * nblk].unsqueeze(2).broadcast_to([128, nblk, 32]),
                ALU.mult), waits=[t1])
            t2 = vop(lambda v: v.tensor_scalar(
                wrow[:, :], wrow[:, :], MAGIC, MAGIC, ALU.add, ALU.subtract),
                waits=[t2])
            t2 = vop(lambda v: v.tensor_tensor(
                wqb[:, :], wrow[:, :],
                wsc[:, 0:nblk].unsqueeze(2).broadcast_to([128, nblk, 32]),
                ALU.mult), waits=[t2])
            for kc in range(nI // 128):
                t_tr = dmat(wsl(kc * O + oc * 128, 128),
                            wqb[:, kc * 128:(kc + 1) * 128],
                            waits=[t2] if kc == 0 else ())
            prev_tr = t_tr
        return t_tr

    def phase_b(which, w_hbm, spill, rm_col):
        sc.barrier()
        ar.reset()
        xnb2 = ar.alloc("xnb2", [128, NKD * TB], BF16)
        xqf = ar.alloc("xqf", [128, NKD * TB], F32)
        xqb = ar.alloc("xqb", [128, NKD * TB], BF16)
        stg = ar.alloc("stg", [128, 4 * TB], F32)
        wrow = ar.alloc("wrow", [128, D], F32)
        wsc = ar.alloc("wsc", [128, 2 * (D // 32)], F32)
        wqb = ar.alloc("wqb", [128, D], BF16)
        t_w = lpbq_load(w_hbm, NKF, D, wrow, wsc, wqb)
        first = True
        for it in range(ntB):
            t0 = it * TB
            if which == "up":
                t_in = dma("sync", sb3(xnb2, NKD), k3(xn_sp, NKD, t0, TB), sem=LDA)
                t1 = sop(lambda s: s.activation(
                    xqf[:, :], xnb2[:, :], ACT.Identity,
                    scale=scal[:, SCL_IS0:SCL_IS0 + 1]), waits=[t_in])
                t2 = vop(lambda v: v.tensor_scalar(
                    xqf[:, :], xqf[:, :], MAGIC, MAGIC, ALU.add, ALU.subtract),
                    waits=[t1])
                t_q = sop(lambda s: s.activation(
                    xqb[:, :], xqf[:, :], ACT.Identity,
                    scale=scal[:, SCL_S0:SCL_S0 + 1]), waits=[t2])
                dma("sync", k3(xq_sp, NKD, t0, TB), sb3(xqb, NKD), waits=[t_q], sem=SPL)
            else:
                t_q = dma("sync", sb3(xqb, NKD), k3(xq_sp, NKD, t0, TB), sem=LDA)
            slots6 = [g1[:, 0:TB], g2p[:, 0:TB], pA[:, 0:TB],
                      pA[:, 512:512 + TB], pB[:, 0:TB], pB[:, 512:512 + TB]]
            ipev = {}
            stgt = {}
            for oc in range(NKF):
                pb = slots6[oc % 6]
                slot = oc % 4
                last = None
                for k in range(NKD):
                    wts = []
                    if k == 0:
                        wts = [t_q] + ([t_w] if first else [])
                        if oc >= 6:
                            wts.append(ipev[oc - 6])
                    last = mm(pb[:, 0:TB], wsl(k * I + oc * 128, 128),
                              xqb[:, k * TB:(k + 1) * TB],
                              start=(k == 0), stop=(k == NKD - 1),
                              waits=wts, inc=TN if k == NKD - 1 else None)
                first = False
                wev = [last]
                if ("stg", slot) in stgt:
                    wev.append(stgt[("stg", slot)])
                ev = vop(lambda v, pb=pb, slot=slot: v.tensor_copy(
                    stg[:, slot * TB:(slot + 1) * TB], pb[:, :]), waits=wev)
                ipev[oc] = ev
                t_mr = vop(lambda v, slot=slot: v.reduce_max(
                    runmax[:, RM_T:RM_T + 1], stg[:, slot * TB:(slot + 1) * TB],
                    axis=AXX, apply_absolute_value=True), waits=[ev])
                if it == 0 and oc == 0:
                    vop(lambda v: v.tensor_copy(
                        runmax[:, rm_col:rm_col + 1], runmax[:, RM_T:RM_T + 1]),
                        waits=[t_mr])
                else:
                    vop(lambda v: v.tensor_max(
                        runmax[:, rm_col:rm_col + 1],
                        runmax[:, rm_col:rm_col + 1],
                        runmax[:, RM_T:RM_T + 1]), waits=[t_mr])
                if slot == 3:
                    td = dma("sync",
                             spill.rearrange("p (c t) -> p c t", c=NKF)[:, oc - 3:oc + 1, t0:t0 + TB],
                             stg[:, :].rearrange("p (c t) -> p c t", c=4),
                             waits=[ipev[oc]], sem=STG)
                    for sl in range(4):
                        stgt[("stg", sl)] = td
            sc.barrier()
        sc.barrier()

    if STOP_PHASE >= 4:
        phase_b("up", wup, up_sp, RM_UP)
    if STOP_PHASE >= 5:
        phase_b("gate", wgate, gate_sp, RM_G)
    if STOP_PHASE >= 5:
        do_cc(1, [(RM_UP, SCL_SUP, SCL_ISUP), (RM_G, SCL_SG, SCL_ISG)])

    if STOP_PHASE >= 6:
        ar.reset()
        big1 = ar.alloc("big1", [128, NKF * TB], F32)
        big2 = ar.alloc("big2", [128, NKF * TB], F32)
        for it in range(ntB):
            t0 = it * TB
            t_in = dma("sync", sb3(big1, NKF), k3(gate_sp, NKF, t0, TB), sem=LDA)
            t1 = vop(lambda v: v.tensor_scalar(
                big2[:, :], big1[:, :], scal[:, SCL_ISG:SCL_ISG + 1], MAGIC,
                ALU.mult, ALU.add), waits=[t_in])
            t1 = vop(lambda v: v.tensor_scalar(
                big2[:, :], big2[:, :], MAGIC, scal[:, SCL_SG:SCL_SG + 1],
                ALU.subtract, ALU.mult), waits=[t1])
            t2 = sop(lambda s: s.activation(big1[:, :], big2[:, :], ACT.Sigmoid),
                     waits=[t1])
            t2 = vop(lambda v: v.tensor_scalar(
                big1[:, :], big1[:, :], 65536.0, MAGIC, ALU.mult, ALU.add),
                waits=[t2])
            t2 = vop(lambda v: v.tensor_scalar(
                big1[:, :], big1[:, :], MAGIC, 65535.0, ALU.subtract, ALU.min),
                waits=[t2])
            t3 = vop(lambda v: v.scalar_tensor_tensor(
                big2[:, :], big2[:, :], 1.0 / 65536.0, big1[:, :],
                ALU.mult, ALU.mult), waits=[t2])
            t_mr = vop(lambda v: v.reduce_max(runmax[:, RM_T:RM_T + 1], big2[:, :],
                                              axis=AXX, apply_absolute_value=True),
                       waits=[t3])
            if it == 0:
                vop(lambda v: v.tensor_copy(runmax[:, RM_G2:RM_G2 + 1],
                                            runmax[:, RM_T:RM_T + 1]), waits=[t_mr])
            else:
                vop(lambda v: v.tensor_max(runmax[:, RM_G2:RM_G2 + 1],
                                           runmax[:, RM_G2:RM_G2 + 1],
                                           runmax[:, RM_T:RM_T + 1]), waits=[t_mr])
            dma("sync", k3(g2_sp, NKF, t0, TB), sb3(big2, NKF), waits=[t3], sem=SPL)
            sc.barrier()
    if STOP_PHASE >= 6:
        do_cc(2, [(RM_G2, SCL_SG2, SCL_ISG2)])

    if STOP_PHASE >= 7:
        for it in range(ntB):
            t0 = it * TB
            t_a1 = dma("sync", sb3(big1, NKF), k3(g2_sp, NKF, t0, TB), sem=LDA)
            t_a2 = dma("sync", sb3(big2, NKF), k3(up_sp, NKF, t0, TB), sem=LDB)
            t1 = vop(lambda v: v.tensor_scalar(
                big1[:, :], big1[:, :], scal[:, SCL_ISG2:SCL_ISG2 + 1], MAGIC,
                ALU.mult, ALU.add), waits=[t_a1])
            t1 = vop(lambda v: v.tensor_scalar(
                big1[:, :], big1[:, :], MAGIC, scal[:, SCL_SG2:SCL_SG2 + 1],
                ALU.subtract, ALU.mult), waits=[t1])
            t2 = vop(lambda v: v.tensor_scalar(
                big2[:, :], big2[:, :], scal[:, SCL_ISUP:SCL_ISUP + 1], MAGIC,
                ALU.mult, ALU.add), waits=[t_a2])
            t2 = vop(lambda v: v.tensor_scalar(
                big2[:, :], big2[:, :], MAGIC, scal[:, SCL_SUP:SCL_SUP + 1],
                ALU.subtract, ALU.mult), waits=[t2])
            t3 = vop(lambda v: v.tensor_mul(big1[:, :], big1[:, :], big2[:, :]),
                     waits=[t1, t2])
            t_mr = vop(lambda v: v.reduce_max(runmax[:, RM_T:RM_T + 1], big1[:, :],
                                              axis=AXX, apply_absolute_value=True),
                       waits=[t3])
            if it == 0:
                vop(lambda v: v.tensor_copy(runmax[:, RM_O:RM_O + 1],
                                            runmax[:, RM_T:RM_T + 1]), waits=[t_mr])
            else:
                vop(lambda v: v.tensor_max(runmax[:, RM_O:RM_O + 1],
                                           runmax[:, RM_O:RM_O + 1],
                                           runmax[:, RM_T:RM_T + 1]), waits=[t_mr])
            dma("sync", k3(o_sp, NKF, t0, TB), sb3(big1, NKF), waits=[t3], sem=SPL)
            sc.barrier()
    if STOP_PHASE >= 7:
        do_cc(3, [(RM_O, SCL_SO, SCL_ISO)])

    if STOP_PHASE >= 8:
        ar.reset()
        wrow = ar.alloc("wrowE", [128, I], F32)
        wsc = ar.alloc("wscE", [128, 2 * (I // 32)], F32)
        wqb = ar.alloc("wqbE", [128, I], BF16)
        t_w = lpbq_load(wdown, NKD, I, wrow, wsc, wqb)
        sc.barrier()
        ar.reset()
        obig = ar.alloc("obig", [128, NKF * TB], F32)
        oqb = ar.alloc("oqb", [128, NKF * TB], BF16)
        gsil = ar.alloc("gsil", [128, NKD * TB], F32)
        sres = ar.alloc("sres", [128, NKD * TB], F32)
        first = True
        for it in range(ntB):
            t0 = it * TB
            t_in = dma("sync", sb3(obig, NKF), k3(o_sp, NKF, t0, TB), sem=LDA)
            t1 = vop(lambda v: v.tensor_scalar(
                obig[:, :], obig[:, :], scal[:, SCL_ISO:SCL_ISO + 1], MAGIC,
                ALU.mult, ALU.add), waits=[t_in])
            t_q = vop(lambda v: v.tensor_scalar(
                oqb[:, :], obig[:, :], MAGIC, scal[:, SCL_SO:SCL_SO + 1],
                ALU.subtract, ALU.mult), waits=[t1])
            t_sl = dma("sync", sb3(sres, NKD), k3(s_sp, NKD, t0, TB), sem=LDB)
            if DEBUG_SPILL:
                dma("sync", k3(oq_dbg, NKF, t0, TB), sb3(oqb, NKF),
                    waits=[t_q], sem=SPL)
                if it == 0:
                    dma("sync", wd_dbg, wsl(0, NKF * D), waits=[t_w], sem=SPL)
            slots6 = [g1[:, 0:TB], g2p[:, 0:TB], pA[:, 0:TB],
                      pA[:, 512:512 + TB], pB[:, 0:TB], pB[:, 512:512 + TB]]
            ipev = {}
            for oc in range(NKD):
                pb = slots6[oc % 6]
                last = None
                for k in range(NKF):
                    wts = []
                    if k == 0:
                        wts = [t_q] + ([t_w] if first else [])
                        if oc >= 6:
                            wts.append(ipev[oc - 6])
                    last = mm(pb[:, 0:TB], wsl(k * D + oc * 128, 128),
                              oqb[:, k * TB:(k + 1) * TB],
                              start=(k == 0), stop=(k == NKF - 1),
                              waits=wts, inc=TN if k == NKF - 1 else None)
                first = False
                ipev[oc] = sop(lambda s, oc=oc, pb=pb: s.activation(
                    gsil[:, oc * TB:(oc + 1) * TB], pb[:, 0:TB], ACT.Silu),
                    waits=[last])
            t_g = ipev[NKD - 1]
            if DEBUG_SPILL:
                dma("sync", k3(g_dbg, NKD, t0, TB), sb3(gsil, NKD), waits=[t_g], sem=SPL)
            t_sg = vop(lambda v: v.tensor_mul(gsil[:, :], gsil[:, :], sres[:, :]),
                       waits=[t_g, t_sl])
            t_x2 = dma("sync", sb3(sres, NKD), k3(x_hbm, NKD, t0, TB),
                       waits=[t_sg], sem=LDC)
            if DEBUG_SPILL:
                dma("sync", k3(sg_dbg, NKD, t0, TB), sb3(gsil, NKD),
                    waits=[t_sg], sem=SPL)
            t_o = vop(lambda v: v.tensor_add(gsil[:, :], gsil[:, :], sres[:, :]),
                      waits=[t_x2])
            dma("sync", k3(out_hbm, NKD, t0, TB), sb3(gsil, NKD), waits=[t_o], sem=SPL)
            sc.barrier()

    sc.barrier()
    for e in Sched.ENGINES:
        sc.emit(e, lambda en: en.engine_nop() if hasattr(en, "engine_nop")
                else en.nop())

    for s in sc.sems:
        s.h = nc.alloc_semaphore(s.name)
    block = ctx.enter_context(nc.Block())

    def run(engine_name):
        def body(e):
            for fn in sc.ops[engine_name]:
                fn(e)
        return body

    block.sync(run("sync"))
    block.gpsimd(run("gpsimd"))
    block.tensor(run("tensor"))
    block.vector(run("vector"))
    block.scalar(run("scalar"))

    ctx.close()
    return nc


def _fm(x2d, nk):
    t = np.ascontiguousarray(x2d.T).reshape(nk, 128, -1)
    return np.ascontiguousarray(t.transpose(1, 0, 2).reshape(128, -1))


def _wstat(w, nk):
    out_d, in_d = w.shape
    t = np.ascontiguousarray(w.T).reshape(nk, 128, out_d)
    return np.ascontiguousarray(t.transpose(1, 0, 2).reshape(128, nk * out_d))


def kernel(L0_attn_X, L0_attn_in_proj_w, L0_attn_conv_w, L0_attn_conv_b,
           L0_attn_dt_bias, L0_attn_A_log, L0_attn_D, L0_attn_norm_w,
           L0_attn_out_proj_w, L0_attn_to_h_w, L0_ffn_w_gate, L0_ffn_w_up,
           L0_ffn_w_down):
    cfg = CFG
    nco = cfg.ncore
    X = np.asarray(L0_attn_X, dtype=np.float32)
    B, L, D = X.shape
    NKD, NKI, NKC, NKF = D // 128, cfg.DI // 128, cfg.CONV // 128, cfg.I // 128

    win = _wstat(np.asarray(L0_attn_in_proj_w, np.float32), NKD)

    def _wrow(w):
        o, i = w.shape
        return np.ascontiguousarray(
            w.reshape(o // 128, 128, i).transpose(1, 0, 2).reshape(128, -1))

    wup = _wrow(np.asarray(L0_ffn_w_up, np.float32))
    wgate = _wrow(np.asarray(L0_ffn_w_gate, np.float32))
    wdown = _wrow(np.asarray(L0_ffn_w_down, np.float32))
    woutp = _wstat(np.asarray(L0_attn_out_proj_w, np.float32), NKI)
    wtoh = _wstat(np.asarray(L0_attn_to_h_w, np.float32)[:D, :], NKD)
    convw = np.ascontiguousarray(
        np.asarray(L0_attn_conv_w, np.float32)[:, 0, :]
        .reshape(NKC, 128, 4).transpose(1, 0, 2).reshape(128, NKC * 4))
    convb = np.ascontiguousarray(
        np.asarray(L0_attn_conv_b, np.float32).reshape(NKC, 128).T)
    alog = np.ascontiguousarray(
        np.tile(np.asarray(L0_attn_A_log, np.float32), 4)[:, None])
    dtb = np.ascontiguousarray(
        np.tile(np.asarray(L0_attn_dt_bias, np.float32), 4)[:, None])
    Dv = np.asarray(L0_attn_D, np.float32)
    dvec = np.empty((128, NKI), np.float32)
    for c in range(NKI):
        dvec[0:64, c] = Dv[2 * c]
        dvec[64:128, c] = Dv[2 * c + 1]
    normw = np.ascontiguousarray(
        np.asarray(L0_attn_norm_w, np.float32).reshape(NKI, 128).T)

    nc = build_program(cfg)

    seq_per_core = B // nco
    in_maps = []
    for c in range(nco):
        xs = X[c * seq_per_core:(c + 1) * seq_per_core].reshape(-1, D)
        in_maps.append({
            "x_fm": _fm(xs, NKD), "win": win, "wup": wup, "wgate": wgate,
            "wdown": wdown, "woutp": woutp, "wtoh": wtoh, "convw": convw,
            "convb": convb, "alog": alog, "dvec": dvec, "dtb": dtb,
            "normw": normw,
        })
    res = run_bass_kernel_spmd(nc, in_maps, core_ids=list(range(nco)))
    outs = []
    for c in range(nco):
        o = res.results[c]["out_fm"]
        o = o.reshape(128, NKD, -1).transpose(1, 0, 2).reshape(D, -1).T
        outs.append(o.reshape(seq_per_core, L, D))
    return np.concatenate(outs, axis=0)
